# version 18
# speedup vs baseline: 1.2429x; 1.0458x over previous
LAST_EM = None
"""Fused raw-Bass megakernel for mk_seq_postln_d1 (mamba mixer + capacity MoE).

Single run_bass_kernel_spmd launch; one program shared by all 8 cores
(core identity enters only through per-core inputs).

Core c: batch b=c//4, d_inner quarter q=c%4 (scan phase), token sub-block q
of every T-token chunk (token phase, via 4-rank AllToAll).

See build_program for the phase structure. Matmul operands fp16 with fp32
PSUM accumulation; scan decay/state fp32; elementwise fp32 except marked.
"""
import numpy as np
from contextlib import ExitStack
import concourse.bass as bass
import concourse.mybir as mybir
from concourse.bass_utils import run_bass_kernel_spmd
from concourse.bass import IndirectOffsetOnAxis
from concourse.aot_env import allow_ici_all_to_all

F32 = mybir.dt.float32
F16 = mybir.dt.float16
I32 = mybir.dt.int32
AF = mybir.ActivationFunctionType
ALU = mybir.AluOpType
AX = mybir.AxisListType.X

DM = 1024
DI = 2048
DS = 16
DTR = 64
NE = 8
CAP = 640
FF = 4096

(C_NEG1, C_ONE, C_EPS, C_NEGHALF, C_INV1024, C_CAP, C_BIG, C_G1, C_G2,
 C_HALF, C_ZERO) = range(11)
NCONST = 11


class Dep:
    __slots__ = ("w", "r")

    def __init__(self):
        self.w = None
        self.r = []

    def rd(self):
        return [self.w] if self.w else []

    def wd(self):
        return ([self.w] if self.w else []) + self.r

    def after_read(self, tok):
        self.r.append(tok)

    def after_write(self, tok):
        self.w = tok
        self.r = []


class Em:
    def __init__(self, nc, ctx):
        self.nc = nc
        self.ctx = ctx
        self.sems = {}
        self.cnt = {}
        self.n_coll = 0
        for name in ("d_sp", "d_gp", "d_act", "e_act", "e_dve", "e_pe", "e_gp", "cc"):
            self.sems[name] = ctx.enter_context(nc.semaphore(name))
            self.cnt[name] = 0
        self.streams = {e: [] for e in ("sp", "gp", "act", "dve", "pe")}
        self.hw = {e: {} for e in ("sp", "gp", "act", "dve", "pe")}
        self.fence = []

    def _add(self, stream, semname, inc, fn, reads=(), writes=(), extra=()):
        waits = {}
        if inc == 16 and self.cnt[semname] > 0:
            waits[semname] = self.cnt[semname]
        for dl in ([d.rd() for d in reads] + [d.wd() for d in writes]
                   + [list(extra), self.fence]):
            for t in dl:
                if t is None:
                    continue
                s, v = t
                waits[s] = max(waits.get(s, 0), v)
        hw = self.hw[stream]
        waits = {s: v for s, v in waits.items() if v > hw.get(s, 0)}
        for s, v in waits.items():
            hw[s] = v
        self.cnt[semname] += inc
        tok = (semname, self.cnt[semname])
        sem = self.sems[semname]
        wl = sorted(waits.items())

        def closure(h, fn=fn, wl=wl, sem=sem, inc=inc):
            for s, v in wl:
                h.wait_ge(self.sems[s], v)
            fn(h).then_inc(sem, inc)
        self.streams[stream].append(closure)
        for d in reads:
            d.after_read(tok)
        for d in writes:
            d.after_write(tok)
        return tok

    def dve(self, fn, reads=(), writes=(), extra=()):
        return self._add("dve", "e_dve", 1, fn, reads, writes, extra)

    def act(self, fn, reads=(), writes=(), extra=()):
        return self._add("act", "e_act", 1, fn, reads, writes, extra)

    def pe(self, fn, reads=(), writes=(), extra=()):
        return self._add("pe", "e_pe", 1, fn, reads, writes, extra)

    def gp(self, fn, reads=(), writes=(), extra=()):
        return self._add("gp", "e_gp", 1, fn, reads, writes, extra)

    def dma(self, q, fn, reads=(), writes=(), extra=()):
        return self._add(q, "d_" + q, 16, fn, reads, writes, extra)

    def coll(self, fn, reads=(), writes=(), extra=(), kind="cc"):
        name = f"cc{self.n_coll}"
        self.n_coll += 1
        self.sems[name] = self.ctx.enter_context(self.nc.semaphore(name))
        self.cnt[name] = 0
        return self._add("gp", name, 1, fn, reads, writes, extra)

    def set_fence(self):
        self.fence = [(n, v) for n, v in self.cnt.items() if v > 0]


def build_program(L, T):
    import os as _os
    _os.environ["TRN_ALLOW_ICI_ALL_TO_ALL"] = "1"
    NCH = L // T
    TQ = T // 4
    TPC = TQ // 128
    NBLK = NCH * TPC
    LQ = L // 4
    GBLK = 8 * NBLK
    NKORD = (GBLK + 127) // 128
    NH = T // 512
    assert TQ % 128 == 0 and TQ <= 512 and NBLK <= 32

    nc = bass.Bass("TRN2", target_bir_lowering=False, debug=False)

    def EI(name, shape, dt):
        return nc.dram_tensor(name, shape, dt, kind="ExternalInput").ap()

    x_rows = EI("x_rows", [L, DM], F32)
    x_own = EI("x_own", [LQ, DM], F32)
    inprojT = EI("inprojT", [128, 8, DM], F32)
    xprojT = EI("xprojT", [128, 4, 96], F32)
    dtwT = EI("dtwT", [DTR, 512], F32)
    convw = EI("convw", [128, 16], F32)
    convb = EI("convb", [128, 4], F32)
    dtb = EI("dtb", [128, 4], F32)
    dpv = EI("dpv", [128, 4], F32)
    alog = EI("alog", [128, 64], F32)
    opwT = EI("opwT", [128, 16, DM], F32)
    gateT = EI("gateT", [128, 8, DM], F32)
    fd1T = EI("fd1T", [128, 8, DM], F32)
    fd2T = EI("fd2T", [128, 8, DM], F32)
    gateb = EI("gateb", [128, 8], F32)
    fdb1 = EI("fdb1", [128, 8], F32)
    fdb2 = EI("fdb2", [128, 8], F32)
    fdlng = EI("fdlng", [128, 8], F32)
    fdlnb = EI("fdlnb", [128, 8], F32)
    normg = EI("normg", [128, 8], F32)
    normb = EI("normb", [128, 8], F32)
    normgr = EI("normgr", [1, DM], F16)
    normbr = EI("normbr", [1, DM], F16)
    fdlngr = EI("fdlngr", [1, DM], F16)
    fdlnbr = EI("fdlnbr", [1, DM], F16)
    bal = EI("bal", [1, 1], F32)
    ffngw = EI("ffngw", [128, 8, 8], F32)
    ffngb = EI("ffngb", [128, 8], F32)
    w1e = EI("w1e", [128, 8, FF], F32)
    b1e = EI("b1e", [128, 32], F32)
    w2e = EI("w2e", [128, 32, DM], F32)
    b2e = EI("b2e", [128, 8], F32)
    ident16_i = EI("ident16_i", [128, 128], F16)
    ident32_i = EI("ident32_i", [128, 128], F32)
    consts_i = EI("consts_i", [128, NCONST], F32)
    evals_i = EI("evals_i", [128, 8], F32)
    ordm_i = EI("ordm_i", [128, NKORD, NBLK], F32)
    slotidx_i = EI("slotidx_i", [128, 16], I32)
    out_core = nc.dram_tensor("out_core", [LQ, DM], F32, kind="ExternalOutput").ap()
    import os as _os2
    DBG = _os2.environ.get("KDBG", "") == "1"
    DBG2 = _os2.environ.get("KDBG2", "igx") if DBG else ""
    if DBG:
        dbg_x1 = nc.dram_tensor("dbg_x1", [NCH, DM, TQ], F16, kind="ExternalOutput").ap()
        dbg_idx = nc.dram_tensor("dbg_idx", [128, NBLK], I32, kind="ExternalOutput").ap()
        dbg_gp = nc.dram_tensor("dbg_gp", [128, NBLK], F32, kind="ExternalOutput").ap()
        dbg_y2i = nc.dram_tensor("dbg_y2i", [NCH, 128, 4 * T], F16,
                                 kind="ExternalOutput").ap()
        dbg_y2g = nc.dram_tensor("dbg_y2g", [NCH, 128, 16 * TQ], F16,
                                 kind="ExternalOutput").ap()
        dbg_xD = nc.dram_tensor("dbg_xD", [NCH, 128, 8 * TQ], F16,
                                kind="ExternalOutput").ap()

    xdbl_in = nc.dram_tensor("xdbl_in", [NCH, 96, T], F16)
    xdbl_out = nc.dram_tensor("xdbl_out", [NCH, 96, T], F16)
    a2a_in_l = [nc.dram_tensor(f"a2a_in{kk}", [8, 512 * TQ], F16) for kk in range(NCH)]
    a2a_out_l = [nc.dram_tensor(f"a2a_out{kk}", [8, 512 * TQ], F16) for kk in range(NCH)]
    x1_d = nc.dram_tensor("x1_d", [NCH, DM, TQ], F16)
    x1tok_d = nc.dram_tensor("x1tok_d", [NCH, TQ, DM], F16)
    cnt_in = nc.dram_tensor("cnt_in", [NBLK, 8], F32)
    cnt_out = nc.dram_tensor("cnt_out", [GBLK, 8], F32)
    off_rt = nc.dram_tensor("off_rt", [1, 32 * 8], F32)
    disp_d = nc.dram_tensor("disp_d", [NE * CAP, DM], F16)
    dispm_d = nc.dram_tensor("dispm_d", [CAP, DM], F16)
    agin_d = nc.dram_tensor("agin_d", [CAP, DM], F16)
    agout_d = nc.dram_tensor("agout_d", [NE * CAP, DM], F16)
    opw_h = nc.dram_tensor("opw_h", [128, 16 * DM], F16)
    flush_d = nc.dram_tensor("flush_d", [1, 16], F16)
    gw_h = nc.dram_tensor("gw_h", [128, 8 * DM], F16)
    f1_h = nc.dram_tensor("f1_h", [128, 8 * DM], F16)
    f2_h = nc.dram_tensor("f2_h", [128, 8 * DM], F16)

    deps = {}

    def D(name):
        if name not in deps:
            deps[name] = Dep()
        return deps[name]

    with ExitStack() as octx:
        em = Em(nc, octx)

        _names = {}

        def sbt(c, name, shape, dt):
            n = _names.get(name, 0)
            _names[name] = n + 1
            if n:
                name = f"{name}__{n}"
            return c.enter_context(nc.sbuf_tensor(name, shape, dt))

        consts = sbt(octx, "consts", [128, NCONST], F32)
        ident16 = sbt(octx, "ident16", [128, 128], F16)
        ident32 = sbt(octx, "ident32", [128, 128], F32)
        ones32 = sbt(octx, "ones32", [1, 128], F32)
        onescol = sbt(octx, "onescol", [128, 1], F32)
        onescol16 = sbt(octx, "onescol16", [128, 1], F16)
        convw_s = sbt(octx, "convw_s", [128, 16], F32)
        convb_s = sbt(octx, "convb_s", [128, 4], F32)
        dtb_s = sbt(octx, "dtb_s", [128, 4], F32)
        dp_s = sbt(octx, "dp_s", [128, 4], F32)
        amat = sbt(octx, "amat", [128, 64], F32)
        gateb_s = sbt(octx, "gateb_s", [128, 8], F32)
        fdb1_s = sbt(octx, "fdb1_s", [128, 8], F32)
        fdb2_s = sbt(octx, "fdb2_s", [128, 8], F32)
        fdlng_s = sbt(octx, "fdlng_s", [128, 8], F32)
        fdlnb_s = sbt(octx, "fdlnb_s", [128, 8], F32)
        normg_s = sbt(octx, "normg_s", [128, 8], F32)
        normb_s = sbt(octx, "normb_s", [128, 8], F32)
        ffngb_s = sbt(octx, "ffngb_s", [128, 8], F32)
        b1_s = sbt(octx, "b1_s", [128, 32], F32)
        b2_s = sbt(octx, "b2_s", [128, 8], F32)
        evals_s = sbt(octx, "evals_s", [128, 8], F32)
        ordm_s = sbt(octx, "ordm_s", [128, NKORD, NBLK], F32)
        bal_s = sbt(octx, "bal_s", [1, 1], F32)
        f_col = sbt(octx, "f_col", [128, 1], F32)
        onef_col = sbt(octx, "onef_col", [128, 1], F32)
        state = sbt(octx, "state", [128, 64], F32)
        halo = sbt(octx, "halo", [128, 4, 3], F16)
        logits_all = sbt(octx, "logits_all", [128, NBLK, 8], F32)
        gprob_all = sbt(octx, "gprob_all", [128, NBLK], F32)
        mask_all = sbt(octx, "mask_all", [128, NBLK, 8], F16)
        idx_all = sbt(octx, "idx_all", [128, NBLK], I32)
        r0 = sbt(octx, "r0", [128, 8 * NBLK], F32)
        r1 = sbt(octx, "r1", [128, 8 * NBLK], F32)
        r2 = sbt(octx, "r2", [128, 8], F32)
        r3 = sbt(octx, "r3", [128, 8], F32)
        off_s = sbt(octx, "off_s", [32, 8], F32)
        gat_s = sbt(octx, "gat_s", [128, NKORD, 8], F32)
        slotidx = sbt(octx, "slotidx", [128, 16], I32)
        ipw16 = sbt(octx, "ipw16", [128, 8, DM], F16)
        xpw16 = sbt(octx, "xpw16", [128, 4, 96], F16)
        dtw16 = sbt(octx, "dtw16", [DTR, 512], F16)
        fgw16 = sbt(octx, "fgw16", [128, 8, 8], F16)
        ngbc = sbt(octx, "ngbc", [128, DM], F16)
        nbbc = sbt(octx, "nbbc", [128, DM], F16)
        fggbc = sbt(octx, "fggbc", [128, DM], F16)
        fgbbc = sbt(octx, "fgbbc", [128, DM], F16)
        gpfl = sbt(octx, "gpfl", [1, 16], F16)

        pb4_t = octx.enter_context(nc.psum_tensor("pb4", [128, 512], F32))
        ptr16 = octx.enter_context(nc.psum_tensor("ptr16", [128, 8, 128], F16))
        ptr16b = octx.enter_context(nc.psum_tensor("ptr16b", [128, 8, 128], F16))
        pb03 = ExitStack()
        pbank = [pb03.enter_context(nc.psum_tensor(f"pb{i}", [128, 512], F32))
                 for i in range(4)]
        pbank.append(pb4_t)

        def cst(col):
            return consts[:, col:col + 1]

        def mm(bank_ap, lhsT_ap, rhs_ap, first, last, rd, wd):
            return em.pe(lambda t: t.matmul(bank_ap, lhsT_ap, rhs_ap,
                                            start=first, stop=last),
                         reads=list(rd), writes=list(wd))

        def ld(dst, src, wdep):
            return em.dma("sp", lambda h: h.dma_start(out=dst, in_=src),
                          writes=[wdep])

        def layer_norm(src, srcdep, dst, dstdep, TT, scr, scrdep, rows, rowsdep,
                       bcst, bcstdep, gb=None, ones_src=None, ones_scr=None):
            if ones_src is None:
                ones_src = onescol
            if ones_scr is None:
                ones_scr = onescol
            assert 2 * TT <= 512
            em.act(lambda s: s.activation(out=scr.ap().rearrange("p a b -> p (a b)"),
                                          in_=src.ap().rearrange("p a b -> p (a b)"),
                                          func=AF.Square),
                   reads=[srcdep], writes=[scrdep])
            bk, bkd = pbank[4], D("pb4")
            for m in range(8):
                mm(bk.ap()[0:1, 0:TT], ones_src[:, :], src.ap()[:, m, :], m == 0, m == 7,
                   [D("onescol"), srcdep], [bkd])
            for m in range(8):
                mm(bk.ap()[0:1, TT:2 * TT], ones_scr[:, :], scr.ap()[:, m, :],
                   m == 0, m == 7, [D("onescol"), scrdep], [bkd])
            em.dve(lambda v: v.tensor_scalar(out=rows[0:1, 0:2 * TT],
                                             in0=bk.ap()[0:1, 0:2 * TT],
                                             scalar1=consts[0:1, C_INV1024:C_INV1024 + 1],
                                             scalar2=None, op0=ALU.mult),
                   reads=[bkd, D("consts")], writes=[rowsdep])
            em.dve(lambda v: v.tensor_tensor(out=rows[0:1, 2 * TT:3 * TT],
                                             in0=rows[0:1, 0:TT], in1=rows[0:1, 0:TT],
                                             op=ALU.mult),
                   reads=[], writes=[rowsdep])
            em.dve(lambda v: v.tensor_tensor(out=rows[0:1, 2 * TT:3 * TT],
                                             in0=rows[0:1, TT:2 * TT],
                                             in1=rows[0:1, 2 * TT:3 * TT],
                                             op=ALU.subtract),
                   reads=[], writes=[rowsdep])
            em.act(lambda s: s.activation(out=rows[0:1, TT:2 * TT],
                                          in_=rows[0:1, 2 * TT:3 * TT], func=AF.Ln,
                                          bias=consts[0:1, C_EPS:C_EPS + 1]),
                   reads=[rowsdep, D("consts")], writes=[rowsdep])
            em.act(lambda s: s.activation(out=rows[0:1, TT:2 * TT],
                                          in_=rows[0:1, TT:2 * TT], func=AF.Exp,
                                          scale=consts[0:1, C_NEGHALF:C_NEGHALF + 1]),
                   reads=[D("consts")], writes=[rowsdep])
            em.dve(lambda v: v.tensor_tensor(out=rows[0:1, 0:TT], in0=rows[0:1, 0:TT],
                                             in1=rows[0:1, TT:2 * TT], op=ALU.mult),
                   reads=[], writes=[rowsdep])
            em.dve(lambda v: v.tensor_scalar(out=rows[0:1, 0:TT], in0=rows[0:1, 0:TT],
                                             scalar1=consts[0:1, C_NEG1:C_NEG1 + 1],
                                             scalar2=None, op0=ALU.mult),
                   reads=[D("consts")], writes=[rowsdep])
            mm(bk.ap()[:, 0:TT], ones32[:, :], rows[0:1, TT:2 * TT], True, True,
               [D("ones32"), rowsdep], [bkd])
            em.dve(lambda v: v.tensor_copy(out=bcst[:, 0:TT], in_=bk.ap()[:, 0:TT]),
                   reads=[bkd], writes=[bcstdep])
            mm(bk.ap()[:, 0:TT], ones32[:, :], rows[0:1, 0:TT], True, True,
               [D("ones32"), rowsdep], [bkd])
            em.dve(lambda v: v.tensor_copy(out=bcst[:, TT:2 * TT], in_=bk.ap()[:, 0:TT]),
                   reads=[bkd], writes=[bcstdep])
            for m in range(8):
                em.dve(lambda v, m=m: v.tensor_tensor(out=scr.ap()[:, m, :],
                                                      in0=src.ap()[:, m, :],
                                                      in1=bcst[:, 0:TT], op=ALU.mult),
                       reads=[srcdep, bcstdep], writes=[scrdep])
                em.dve(lambda v, m=m: v.tensor_tensor(out=scr.ap()[:, m, :],
                                                      in0=scr.ap()[:, m, :],
                                                      in1=bcst[:, TT:2 * TT], op=ALU.add),
                       reads=[bcstdep], writes=[scrdep])
                if gb is None:
                    em.dve(lambda v, m=m: v.tensor_copy(out=dst.ap()[:, m, :],
                                                        in_=scr.ap()[:, m, :]),
                           reads=[scrdep], writes=[dstdep])
                else:
                    g_s, gdep, b_s, bdep = gb
                    em.dve(lambda v, m=m: v.tensor_scalar(
                        out=dst.ap()[:, m, :], in0=scr.ap()[:, m, :],
                        scalar1=g_s[:, m:m + 1], scalar2=b_s[:, m:m + 1],
                        op0=ALU.mult, op1=ALU.add),
                        reads=[scrdep, gdep, bdep], writes=[dstdep])

        bregs = {}

        def _mkregs(h):
            bregs[8 * 512 - 1] = h.to_reg(8 * 512 - 1)
            bregs[NE * CAP - 1] = h.to_reg(NE * CAP - 1)
            return h.nop()
        em.gp(_mkregs)

        ld(consts[:, :], consts_i, D("consts"))
        ld(ident16[:, :], ident16_i, D("ident16"))
        ld(ident32[:, :], ident32_i, D("ident32"))
        ld(convw_s[:, :], convw, D("convw"))
        ld(convb_s[:, :], convb, D("convb"))
        ld(dtb_s[:, :], dtb, D("dtb"))
        ld(dp_s[:, :], dpv, D("dp"))
        ld(gateb_s[:, :], gateb, D("gateb"))
        ld(fdb1_s[:, :], fdb1, D("fdb1"))
        ld(fdb2_s[:, :], fdb2, D("fdb2"))
        ld(fdlng_s[:, :], fdlng, D("fdlng"))
        ld(fdlnb_s[:, :], fdlnb, D("fdlnb"))
        ld(normg_s[:, :], normg, D("normg"))
        ld(normb_s[:, :], normb, D("normb"))
        ld(ffngb_s[:, :], ffngb, D("ffngb"))
        ld(b1_s[:, :], b1e, D("b1"))
        ld(b2_s[:, :], b2e, D("b2"))
        ld(evals_s[:, :], evals_i, D("evals"))
        ld(ordm_s.ap().rearrange("p a b -> p (a b)"),
           ordm_i.rearrange("p a b -> p (a b)"), D("ordm"))
        ld(bal_s[:, :], bal, D("bal"))
        ld(slotidx[:, :], slotidx_i, D("slotidx"))
        em.dma("sp", lambda h: h.dma_start(
            out=ngbc.ap(),
            in_=normgr[0:1, :].squeeze(0).partition_broadcast(128)),
            writes=[D("ngbc")])
        em.dma("sp", lambda h: h.dma_start(
            out=nbbc.ap(),
            in_=normbr[0:1, :].squeeze(0).partition_broadcast(128)),
            writes=[D("nbbc")])
        em.dma("sp", lambda h: h.dma_start(
            out=fggbc.ap(),
            in_=fdlngr[0:1, :].squeeze(0).partition_broadcast(128)),
            writes=[D("fggbc")])
        em.dma("sp", lambda h: h.dma_start(
            out=fgbbc.ap(),
            in_=fdlnbr[0:1, :].squeeze(0).partition_broadcast(128)),
            writes=[D("fgbbc")])
        ld(amat[:, :], alog, D("amat"))

        em.act(lambda s: s.activation(out=amat[:, :], in_=amat[:, :], func=AF.Exp),
               reads=[], writes=[D("amat")])
        em.dve(lambda v: v.tensor_scalar(out=amat[:, :], in0=amat[:, :],
                                         scalar1=cst(C_NEG1), scalar2=None,
                                         op0=ALU.mult),
               reads=[D("consts")], writes=[D("amat")])
        em.dve(lambda v: v.memset(state[:, :], 0.0), writes=[D("state")])
        em.dve(lambda v: v.memset(ones32[:, :], 1.0), writes=[D("ones32")])
        em.dve(lambda v: v.memset(onescol[:, :], 1.0), writes=[D("onescol")])
        em.dve(lambda v: v.memset(onescol16[:, :], 1.0), writes=[D("onescol")])
        em.act(lambda s: s.activation(out=bal_s[:, :], in_=bal_s[:, :],
                                      func=AF.Sigmoid),
               reads=[], writes=[D("bal")])
        mm(pbank[0].ap()[:, 0:1], ones32[:, :], bal_s[:, :], True, True,
           [D("ones32"), D("bal")], [D("pb0")])
        em.dve(lambda v: v.tensor_copy(out=f_col[:, :], in_=pbank[0].ap()[:, 0:1]),
               reads=[D("pb0")], writes=[D("fcol")])
        em.dve(lambda v: v.tensor_scalar(out=onef_col[:, :], in0=f_col[:, :],
                                         scalar1=cst(C_NEG1), scalar2=cst(C_ONE),
                                         op0=ALU.mult, op1=ALU.add),
               reads=[D("fcol"), D("consts")], writes=[D("onefcol")])

        with ExitStack() as cW:
            wstage = sbt(cW, "wstage", [128, 8 * DM], F32)
            w16st = sbt(cW, "w16st", [128, 8 * DM], F16)

            def wcast_sb(dst16_flat, src_flat, nfree, depname):
                em.dma("sp", lambda h: h.dma_start(out=wstage.ap()[:, 0:nfree],
                                                   in_=src_flat),
                       writes=[D("wstage")])
                em.dve(lambda v: v.tensor_copy(out=dst16_flat,
                                               in_=wstage.ap()[:, 0:nfree]),
                       reads=[D("wstage")], writes=[D(depname)])

            def wcast_dram(dst_dram_slice, src_flat, nfree):
                em.dma("sp", lambda h: h.dma_start(out=wstage.ap()[:, 0:nfree],
                                                   in_=src_flat),
                       writes=[D("wstage")])
                em.dve(lambda v: v.tensor_copy(out=w16st.ap()[:, 0:nfree],
                                               in_=wstage.ap()[:, 0:nfree]),
                       reads=[D("wstage")], writes=[D("w16st")])
                em.dma("sp", lambda h: h.dma_start(out=dst_dram_slice,
                                                   in_=w16st.ap()[:, 0:nfree]),
                       reads=[D("w16st")], writes=[D("whbm")])

            fl = lambda t: t.ap().rearrange("p a b -> p (a b)")
            wcast_sb(fl(ipw16), inprojT.rearrange("p a b -> p (a b)"), 8 * DM, "ipw16")
            wcast_sb(fl(xpw16), xprojT.rearrange("p a b -> p (a b)"), 4 * 96, "xpw16")
            wcast_sb(fl(fgw16), ffngw.rearrange("p a b -> p (a b)"), 8 * 8, "fgw16")
            em.dma("sp", lambda h: h.dma_start(out=wstage.ap()[0:DTR, 0:512], in_=dtwT),
                   writes=[D("wstage")])
            em.dve(lambda v: v.tensor_copy(out=dtw16[:, :],
                                           in_=wstage.ap()[0:DTR, 0:512]),
                   reads=[D("wstage")], writes=[D("dtw16")])
            opwf = opwT.rearrange("p a b -> p (a b)")
            wcast_dram(opw_h.ap()[:, 0:8 * DM], opwf[:, 0:8 * DM], 8 * DM)
            wcast_dram(opw_h.ap()[:, 8 * DM:16 * DM], opwf[:, 8 * DM:16 * DM], 8 * DM)
            wcast_dram(gw_h.ap(), gateT.rearrange("p a b -> p (a b)"), 8 * DM)
            wcast_dram(f1_h.ap(), fd1T.rearrange("p a b -> p (a b)"), 8 * DM)
            wcast_dram(f2_h.ap(), fd2T.rearrange("p a b -> p (a b)"), 8 * DM)
        em.set_fence()

        with ExitStack() as cL:
            xstage = sbt(cL, "xstage", [128, DM], F32)
            xstage16 = sbt(cL, "xstage16", [128, DM], F16)
            xT16 = sbt(cL, "xT16", [128, 8, T], F16)
            xi1 = sbt(cL, "xi1", [128, T + 3], F16)
            z16_ = [sbt(cL, f"z16_{i}", [128, 4, T], F16) for i in range(2)]
            xc16_ = [sbt(cL, f"xc16_{i}", [128, 4, T], F16) for i in range(2)]
            cacc0 = sbt(cL, "cacc0", [128, T], F32)
            caccB = sbt(cL, "caccB", [128, T], F32)
            delta = sbt(cL, "delta", [128, 4, T], F16)
            w16 = sbt(cL, "w16", [128, 4, T], F16)
            dt16 = sbt(cL, "dt16", [DTR, T], F16)
            bbc = [sbt(cL, f"bbc{i}", [128, T], F16) for i in range(2)]
            cbc = [sbt(cL, f"cbc{i}", [128, T], F16) for i in range(2)]
            a_t = [sbt(cL, f"a_t{i}", [128, T], F32) for i in range(2)]
            b_t = [sbt(cL, f"b_t{i}", [128, T], F16) for i in range(2)]
            h_t = [sbt(cL, f"h_t{i}", [128, T], F16) for i in range(2)]
            ys = sbt(cL, "ys", [128, 4, T], F16)
            wD16 = sbt(cL, "wD16", [128, 8, DM], F16)
            y2tok = sbt(cL, "y2tok", [128, 16, TQ], F16)
            xD = sbt(cL, "xD", [128, 8, TQ], F16)
            m16 = sbt(cL, "m16", [128, 8, TQ], F16)
            g16 = sbt(cL, "g16", [128, 8, TQ], F16)
            dbr = sbt(cL, "dbr", [128, 8, TQ], F16)
            comb = sbt(cL, "comb", [128, 8, TQ], F16)
            t_a = sbt(cL, "t_a", [128, 8, TQ], F16)
            xtok16 = sbt(cL, "xtok16", [128, TPC, DM], F16)
            lnc_cl = sbt(cL, "lnc", [128, 16], F32)
            rows2 = sbt(cL, "rows2", [1, 3 * TQ], F32)
            bcst2 = sbt(cL, "bcst2", [128, 2 * TQ], F16)
            xstD = sbt(cL, "xstD", [128, DM], F32)

            ar_tok = {}

            def emit_A(k):
                pk = k % 2
                xc16, z16 = xc16_[pk], z16_[pk]
                Dxc, Dz = D(f"xc16{pk}"), D(f"z16{pk}")
                for g8 in range(T // 128):
                    tok0 = k * T + g8 * 128
                    em.dma("sp", lambda h, tok0=tok0: h.dma_start(
                        out=xstage.ap(), in_=x_rows[tok0:tok0 + 128, :]),
                        writes=[D("xstage")])
                    em.dve(lambda v: v.tensor_copy(out=xstage16.ap(), in_=xstage.ap()),
                           reads=[D("xstage")], writes=[D("xstage16")])
                    for dt in range(8):
                        em.pe(lambda t, dt=dt: t.transpose(
                            ptr16.ap()[:, dt, :], xstage16[:, dt * 128:(dt + 1) * 128],
                            ident16[:, :]),
                            reads=[D("xstage16"), D("ident16")], writes=[D("ptr16")])
                    em.dve(lambda v, g8=g8: v.tensor_copy(
                        out=xT16.ap()[:, :, g8 * 128:(g8 + 1) * 128], in_=ptr16.ap()),
                        reads=[D("ptr16")], writes=[D("xT16")])

                if k == 0:
                    em.dve(lambda v: v.memset(halo.ap().rearrange("p a b -> p (a b)"), 0.0),
                           writes=[D("halo")])
                for m in range(8):
                    for nh in range(NH):
                        bi = (m * NH + nh) % 4
                        bk, bkd = pbank[bi], D(f"pb{bi}")
                        for kk in range(8):
                            mm(bk.ap(), ipw16[:, kk, m * 128:(m + 1) * 128],
                               xT16[:, kk, nh * 512:(nh + 1) * 512], kk == 0, kk == 7,
                               [D("ipw16"), D("xT16")], [bkd])
                        if m < 4:
                            em.dve(lambda v, m=m, nh=nh, bk=bk: v.tensor_copy(
                                out=xi1.ap()[:, 3 + nh * 512:3 + (nh + 1) * 512],
                                in_=bk.ap()),
                                reads=[bkd], writes=[D("xi1")])
                        else:
                            em.act(lambda s, m=m, nh=nh, bk=bk, z16=z16: s.activation(
                                out=z16.ap()[:, m - 4, nh * 512:(nh + 1) * 512],
                                in_=bk.ap(), func=AF.Silu),
                                reads=[bkd], writes=[Dz])
                    if m < 4:
                        d = m
                        em.dve(lambda v, d=d: v.tensor_copy(
                            out=xi1.ap()[:, 0:3], in_=halo.ap()[:, d, :]),
                            reads=[D("halo")], writes=[D("xi1")])
                        em.dve(lambda v, d=d: v.tensor_copy(
                            out=halo.ap()[:, d, :], in_=xi1.ap()[:, T:T + 3]),
                            reads=[D("xi1")], writes=[D("halo")])
                        em.dve(lambda v, d=d: v.tensor_scalar(
                            out=cacc0[:, :], in0=xi1.ap()[:, 0:T],
                            scalar1=convw_s[:, d * 4:d * 4 + 1], scalar2=None,
                            op0=ALU.mult),
                            reads=[D("xi1"), D("convw")], writes=[D("cacc0")])
                        for tt in (1, 2, 3):
                            em.dve(lambda v, d=d, tt=tt: v.scalar_tensor_tensor(
                                out=cacc0[:, :], in0=xi1.ap()[:, tt:tt + T],
                                scalar=convw_s[:, d * 4 + tt:d * 4 + tt + 1],
                                in1=cacc0[:, :], op0=ALU.mult, op1=ALU.add),
                                reads=[D("xi1"), D("convw")], writes=[D("cacc0")])
                        em.act(lambda s, d=d, xc16=xc16: s.activation(
                            out=xc16.ap()[:, d, :], in_=cacc0[:, :], func=AF.Silu,
                            bias=convb_s[:, d:d + 1]),
                            reads=[D("cacc0"), D("convb")], writes=[Dxc])

                for nh in range(NH):
                    bk, bkd = pbank[4], D("pb4")
                    for d in range(4):
                        mm(bk.ap()[0:96, :], xpw16[:, d, :],
                           xc16[:, d, nh * 512:(nh + 1) * 512], d == 0, d == 3,
                           [D("xpw16"), Dxc], [bkd])
                    em.dve(lambda v, nh=nh, bk=bk: v.tensor_copy(
                        out=xstage16.ap()[0:96, nh * 512:(nh + 1) * 512],
                        in_=bk.ap()[0:96, :]),
                        reads=[bkd], writes=[D("xstage16")])
                em.dma("sp", lambda h, k=k: h.dma_start(out=xdbl_in.ap()[k],
                                                        in_=xstage16.ap()[0:96, :]),
                       reads=[D("xstage16")], writes=[D(f"xdbl_in{k}")])

            def emit_B1(k):
                pk = k % 2
                xc16, z16 = xc16_[pk], z16_[pk]
                Dxc, Dz = D(f"xc16{pk}"), D(f"z16{pk}")
                em.dma("sp", lambda h, k=k: h.dma_start(
                    out=dt16[:, :], in_=xdbl_out.ap()[k][0:DTR, :]),
                    reads=[D(f"xdbl_out{k}")], writes=[D("dt16")])

                for d in range(4):
                    bk, bkd = pbank[4], D("pb4")
                    for nh in range(NH):
                        mm(bk.ap(), dtw16[:, d * 128:(d + 1) * 128],
                           dt16[:, nh * 512:(nh + 1) * 512], True, True,
                           [D("dtw16"), D("dt16")], [bkd])
                        em.act(lambda s, d=d, nh=nh, bk=bk: s.activation(
                            out=caccB[:, nh * 512:(nh + 1) * 512], in_=bk.ap(),
                            func=AF.Identity, bias=dtb_s[:, d:d + 1]),
                            reads=[bkd, D("dtb")], writes=[D("caccB")])
                    em.act(lambda s, d=d: s.activation(out=delta.ap()[:, d, :],
                                                       in_=caccB[:, :], func=AF.Abs),
                           reads=[D("caccB")], writes=[D("delta")])
                    em.act(lambda s, d=d: s.activation(out=delta.ap()[:, d, :],
                                                       in_=delta.ap()[:, d, :],
                                                       func=AF.Exp, scale=cst(C_NEG1)),
                           reads=[D("consts")], writes=[D("delta")])
                    em.act(lambda s, d=d: s.activation(out=delta.ap()[:, d, :],
                                                       in_=delta.ap()[:, d, :],
                                                       func=AF.Ln, bias=cst(C_ONE)),
                           reads=[D("consts")], writes=[D("delta")])
                    em.dve(lambda v: v.tensor_scalar(out=caccB[:, :], in0=caccB[:, :],
                                                     scalar1=cst(C_ZERO), scalar2=None,
                                                     op0=ALU.max),
                           reads=[D("consts")], writes=[D("caccB")])
                    em.dve(lambda v, d=d: v.tensor_tensor(out=delta.ap()[:, d, :],
                                                          in0=delta.ap()[:, d, :],
                                                          in1=caccB[:, :], op=ALU.add),
                           reads=[D("caccB")], writes=[D("delta")])
                    em.dve(lambda v, d=d, xc16=xc16: v.tensor_tensor(
                        out=w16.ap()[:, d, :], in0=delta.ap()[:, d, :],
                        in1=xc16.ap()[:, d, :], op=ALU.mult),
                           reads=[D("delta"), Dxc], writes=[D("w16")])

            def emit_B2(k):
                pk = k % 2
                xc16, z16 = xc16_[pk], z16_[pk]
                Dxc, Dz = D(f"xc16{pk}"), D(f"z16{pk}")
                for sg in range(DS):
                    pp = sg % 2
                    em.dma("sp", lambda h, k=k, sg=sg, pp=pp: h.dma_start(
                        out=bbc[pp][:, :],
                        in_=xdbl_out.ap()[k][DTR + sg:DTR + sg + 1, :]
                        .squeeze(0).partition_broadcast(128)),
                        reads=[D(f"xdbl_out{k}")], writes=[D(f"bbc{pp}")])
                    em.dma("sp", lambda h, k=k, sg=sg, pp=pp: h.dma_start(
                        out=cbc[pp][:, :],
                        in_=xdbl_out.ap()[k][DTR + 16 + sg:DTR + 17 + sg, :]
                        .squeeze(0).partition_broadcast(128)),
                        reads=[D(f"xdbl_out{k}")], writes=[D(f"cbc{pp}")])
                    for d in range(4):
                        col = d * 16 + sg
                        qp = (sg * 4 + d) % 2
                        em.act(lambda s, d=d, col=col, qp=qp: s.activation(
                            out=a_t[qp][:, :], in_=delta.ap()[:, d, :], func=AF.Exp,
                            scale=amat[:, col:col + 1]),
                            reads=[D("delta"), D("amat")], writes=[D(f"a{qp}")])
                        em.dve(lambda v, d=d, qp=qp, pp=pp: v.tensor_tensor(
                            out=b_t[qp][:, :], in0=w16.ap()[:, d, :], in1=bbc[pp][:, :],
                            op=ALU.mult),
                            reads=[D("w16"), D(f"bbc{pp}")], writes=[D(f"b{qp}")])
                        em.dve(lambda v, col=col, qp=qp: v.tensor_tensor_scan(
                            out=h_t[qp][:, :], data0=a_t[qp][:, :], data1=b_t[qp][:, :],
                            initial=state[:, col:col + 1], op0=ALU.mult, op1=ALU.add),
                            reads=[D(f"a{qp}"), D(f"b{qp}"), D("state")],
                            writes=[D(f"h{qp}")])
                        em.dve(lambda v, col=col, qp=qp: v.tensor_copy(
                            out=state[:, col:col + 1], in_=h_t[qp][:, T - 1:T]),
                            reads=[D(f"h{qp}")], writes=[D("state")])
                        if sg == 0:
                            em.dve(lambda v, d=d, qp=qp, pp=pp: v.tensor_tensor(
                                out=ys.ap()[:, d, :], in0=h_t[qp][:, :],
                                in1=cbc[pp][:, :], op=ALU.mult),
                                reads=[D(f"h{qp}"), D(f"cbc{pp}")], writes=[D("ys")])
                        else:
                            em.dve(lambda v, qp=qp, pp=pp: v.tensor_tensor(
                                out=b_t[qp][:, :], in0=h_t[qp][:, :],
                                in1=cbc[pp][:, :], op=ALU.mult),
                                reads=[D(f"h{qp}"), D(f"cbc{pp}")],
                                writes=[D(f"b{qp}")])
                            em.dve(lambda v, d=d, qp=qp: v.tensor_tensor(
                                out=ys.ap()[:, d, :], in0=ys.ap()[:, d, :],
                                in1=b_t[qp][:, :], op=ALU.add),
                                reads=[D(f"b{qp}")], writes=[D("ys")])

                for d in range(4):
                    em.dve(lambda v, d=d, xc16=xc16: v.scalar_tensor_tensor(
                        out=caccB[:, :], in0=xc16.ap()[:, d, :], scalar=dp_s[:, d:d + 1],
                        in1=ys.ap()[:, d, :], op0=ALU.mult, op1=ALU.add),
                        reads=[Dxc, D("dp"), D("ys")], writes=[D("caccB")])
                    em.dve(lambda v, d=d, z16=z16: v.tensor_tensor(
                        out=ys.ap()[:, d, :], in0=caccB[:, :],
                        in1=z16.ap()[:, d, :], op=ALU.mult),
                        reads=[D("caccB"), Dz], writes=[D("ys")])
                for j in range(8):
                    em.dma("sp", lambda h, k=k, j=j: h.dma_start(
                        out=a2a_in_l[k].ap()[j:j + 1, :].rearrange(
                            "o (d p t) -> (o p) d t", p=128, d=4),
                        in_=ys.ap()[:, :, (j % 4) * TQ:(j % 4 + 1) * TQ]),
                        reads=[D("ys")], writes=[D(f"a2a_in{k}")])
                if DBG and "i" in DBG2:
                    em.dma("sp", lambda h, k=k: h.dma_start(
                        out=dbg_y2i[k],
                        in_=ys.ap().rearrange("p a b -> p (a b)")),
                        reads=[D("ys")], writes=[D("dbg")])

            def mmset8(dst, dstname, rhs_of_k, rhsdep, nk, wsrc, wdep_extra,
                       act_fn=AF.Copy, bias=None, biasdep=None):
                npc = nk // 8
                for w in range(npc):
                    em.dma("gp", lambda h, w=w: h.dma_start(
                        out=wD16.ap().rearrange("p a b -> p (a b)"),
                        in_=wsrc(w)),
                        reads=[D("whbm")], writes=[D("wD16")],
                        extra=wdep_extra)
                    for m in range(8):
                        bk, bkd = pbank[m % 4], D(f"pb{m % 4}")
                        off = (m // 4) * TQ
                        for kk in range(8):
                            mm(bk.ap()[:, off:off + TQ],
                               wD16[:, kk, m * 128:(m + 1) * 128],
                               rhs_of_k(w * 8 + kk),
                               w == 0 and kk == 0, w == npc - 1 and kk == 7,
                               [D("wD16"), rhsdep], [bkd])
                for m in range(8):
                    bk, bkd = pbank[m % 4], D(f"pb{m % 4}")
                    off = (m // 4) * TQ
                    if bias is not None:
                        em.act(lambda s, m=m, bk=bk, off=off: s.activation(
                            out=dst.ap()[:, m, :], in_=bk.ap()[:, off:off + TQ],
                            func=act_fn, bias=bias[:, m:m + 1]),
                            reads=[bkd, biasdep], writes=[D(dstname)])
                    else:
                        em.act(lambda s, m=m, bk=bk, off=off: s.activation(
                            out=dst.ap()[:, m, :], in_=bk.ap()[:, off:off + TQ],
                            func=act_fn),
                            reads=[bkd], writes=[D(dstname)])

            def emit_A(k):
                pk = k % 2
                xc16, z16 = xc16_[pk], z16_[pk]
                Dxc, Dz = D(f"xc16{pk}"), D(f"z16{pk}")
                for g8 in range(T // 128):
                    tok0 = k * T + g8 * 128
                    em.dma("sp", lambda h, tok0=tok0: h.dma_start(
                        out=xstage.ap(), in_=x_rows[tok0:tok0 + 128, :]),
                        writes=[D("xstage")])
                    em.dve(lambda v: v.tensor_copy(out=xstage16.ap(), in_=xstage.ap()),
                           reads=[D("xstage")], writes=[D("xstage16")])
                    for dt in range(8):
                        em.pe(lambda t, dt=dt: t.transpose(
                            ptr16.ap()[:, dt, :], xstage16[:, dt * 128:(dt + 1) * 128],
                            ident16[:, :]),
                            reads=[D("xstage16"), D("ident16")], writes=[D("ptr16")])
                    em.dve(lambda v, g8=g8: v.tensor_copy(
                        out=xT16.ap()[:, :, g8 * 128:(g8 + 1) * 128], in_=ptr16.ap()),
                        reads=[D("ptr16")], writes=[D("xT16")])

                if k == 0:
                    em.dve(lambda v: v.memset(halo.ap().rearrange("p a b -> p (a b)"), 0.0),
                           writes=[D("halo")])
                for m in range(8):
                    for nh in range(NH):
                        bi = (m * NH + nh) % 4
                        bk, bkd = pbank[bi], D(f"pb{bi}")
                        for kk in range(8):
                            mm(bk.ap(), ipw16[:, kk, m * 128:(m + 1) * 128],
                               xT16[:, kk, nh * 512:(nh + 1) * 512], kk == 0, kk == 7,
                               [D("ipw16"), D("xT16")], [bkd])
                        if m < 4:
                            em.dve(lambda v, m=m, nh=nh, bk=bk: v.tensor_copy(
                                out=xi1.ap()[:, 3 + nh * 512:3 + (nh + 1) * 512],
                                in_=bk.ap()),
                                reads=[bkd], writes=[D("xi1")])
                        else:
                            em.act(lambda s, m=m, nh=nh, bk=bk, z16=z16: s.activation(
                                out=z16.ap()[:, m - 4, nh * 512:(nh + 1) * 512],
                                in_=bk.ap(), func=AF.Silu),
                                reads=[bkd], writes=[Dz])
                    if m < 4:
                        d = m
                        em.dve(lambda v, d=d: v.tensor_copy(
                            out=xi1.ap()[:, 0:3], in_=halo.ap()[:, d, :]),
                            reads=[D("halo")], writes=[D("xi1")])
                        em.dve(lambda v, d=d: v.tensor_copy(
                            out=halo.ap()[:, d, :], in_=xi1.ap()[:, T:T + 3]),
                            reads=[D("xi1")], writes=[D("halo")])
                        em.dve(lambda v, d=d: v.tensor_scalar(
                            out=cacc0[:, :], in0=xi1.ap()[:, 0:T],
                            scalar1=convw_s[:, d * 4:d * 4 + 1], scalar2=None,
                            op0=ALU.mult),
                            reads=[D("xi1"), D("convw")], writes=[D("cacc0")])
                        for tt in (1, 2, 3):
                            em.dve(lambda v, d=d, tt=tt: v.scalar_tensor_tensor(
                                out=cacc0[:, :], in0=xi1.ap()[:, tt:tt + T],
                                scalar=convw_s[:, d * 4 + tt:d * 4 + tt + 1],
                                in1=cacc0[:, :], op0=ALU.mult, op1=ALU.add),
                                reads=[D("xi1"), D("convw")], writes=[D("cacc0")])
                        em.act(lambda s, d=d, xc16=xc16: s.activation(
                            out=xc16.ap()[:, d, :], in_=cacc0[:, :], func=AF.Silu,
                            bias=convb_s[:, d:d + 1]),
                            reads=[D("cacc0"), D("convb")], writes=[Dxc])

                for nh in range(NH):
                    bk, bkd = pbank[4], D("pb4")
                    for d in range(4):
                        mm(bk.ap()[0:96, :], xpw16[:, d, :],
                           xc16[:, d, nh * 512:(nh + 1) * 512], d == 0, d == 3,
                           [D("xpw16"), Dxc], [bkd])
                    em.dve(lambda v, nh=nh, bk=bk: v.tensor_copy(
                        out=xstage16.ap()[0:96, nh * 512:(nh + 1) * 512],
                        in_=bk.ap()[0:96, :]),
                        reads=[bkd], writes=[D("xstage16")])
                em.dma("sp", lambda h, k=k: h.dma_start(out=xdbl_in.ap()[k],
                                                        in_=xstage16.ap()[0:96, :]),
                       reads=[D("xstage16")], writes=[D(f"xdbl_in{k}")])

            def emit_B1(k):
                pk = k % 2
                xc16, z16 = xc16_[pk], z16_[pk]
                Dxc, Dz = D(f"xc16{pk}"), D(f"z16{pk}")
                em.dma("sp", lambda h, k=k: h.dma_start(
                    out=dt16[:, :], in_=xdbl_out.ap()[k][0:DTR, :]),
                    reads=[D(f"xdbl_out{k}")], writes=[D("dt16")])

                for d in range(4):
                    bk, bkd = pbank[4], D("pb4")
                    for nh in range(NH):
                        mm(bk.ap(), dtw16[:, d * 128:(d + 1) * 128],
                           dt16[:, nh * 512:(nh + 1) * 512], True, True,
                           [D("dtw16"), D("dt16")], [bkd])
                        em.act(lambda s, d=d, nh=nh, bk=bk: s.activation(
                            out=caccB[:, nh * 512:(nh + 1) * 512], in_=bk.ap(),
                            func=AF.Identity, bias=dtb_s[:, d:d + 1]),
                            reads=[bkd, D("dtb")], writes=[D("caccB")])
                    em.act(lambda s, d=d: s.activation(out=delta.ap()[:, d, :],
                                                       in_=caccB[:, :], func=AF.Abs),
                           reads=[D("caccB")], writes=[D("delta")])
                    em.act(lambda s, d=d: s.activation(out=delta.ap()[:, d, :],
                                                       in_=delta.ap()[:, d, :],
                                                       func=AF.Exp, scale=cst(C_NEG1)),
                           reads=[D("consts")], writes=[D("delta")])
                    em.act(lambda s, d=d: s.activation(out=delta.ap()[:, d, :],
                                                       in_=delta.ap()[:, d, :],
                                                       func=AF.Ln, bias=cst(C_ONE)),
                           reads=[D("consts")], writes=[D("delta")])
                    em.dve(lambda v: v.tensor_scalar(out=caccB[:, :], in0=caccB[:, :],
                                                     scalar1=cst(C_ZERO), scalar2=None,
                                                     op0=ALU.max),
                           reads=[D("consts")], writes=[D("caccB")])
                    em.dve(lambda v, d=d: v.tensor_tensor(out=delta.ap()[:, d, :],
                                                          in0=delta.ap()[:, d, :],
                                                          in1=caccB[:, :], op=ALU.add),
                           reads=[D("caccB")], writes=[D("delta")])
                    em.dve(lambda v, d=d, xc16=xc16: v.tensor_tensor(
                        out=w16.ap()[:, d, :], in0=delta.ap()[:, d, :],
                        in1=xc16.ap()[:, d, :], op=ALU.mult),
                           reads=[D("delta"), Dxc], writes=[D("w16")])

            def emit_B2(k):
                pk = k % 2
                xc16, z16 = xc16_[pk], z16_[pk]
                Dxc, Dz = D(f"xc16{pk}"), D(f"z16{pk}")
                for sg in range(DS):
                    pp = sg % 2
                    em.dma("sp", lambda h, k=k, sg=sg, pp=pp: h.dma_start(
                        out=bbc[pp][:, :],
                        in_=xdbl_out.ap()[k][DTR + sg:DTR + sg + 1, :]
                        .squeeze(0).partition_broadcast(128)),
                        reads=[D(f"xdbl_out{k}")], writes=[D(f"bbc{pp}")])
                    em.dma("sp", lambda h, k=k, sg=sg, pp=pp: h.dma_start(
                        out=cbc[pp][:, :],
                        in_=xdbl_out.ap()[k][DTR + 16 + sg:DTR + 17 + sg, :]
                        .squeeze(0).partition_broadcast(128)),
                        reads=[D(f"xdbl_out{k}")], writes=[D(f"cbc{pp}")])
                    for d in range(4):
                        col = d * 16 + sg
                        qp = (sg * 4 + d) % 2
                        em.act(lambda s, d=d, col=col, qp=qp: s.activation(
                            out=a_t[qp][:, :], in_=delta.ap()[:, d, :], func=AF.Exp,
                            scale=amat[:, col:col + 1]),
                            reads=[D("delta"), D("amat")], writes=[D(f"a{qp}")])
                        em.dve(lambda v, d=d, qp=qp, pp=pp: v.tensor_tensor(
                            out=b_t[qp][:, :], in0=w16.ap()[:, d, :], in1=bbc[pp][:, :],
                            op=ALU.mult),
                            reads=[D("w16"), D(f"bbc{pp}")], writes=[D(f"b{qp}")])
                        em.dve(lambda v, col=col, qp=qp: v.tensor_tensor_scan(
                            out=h_t[qp][:, :], data0=a_t[qp][:, :], data1=b_t[qp][:, :],
                            initial=state[:, col:col + 1], op0=ALU.mult, op1=ALU.add),
                            reads=[D(f"a{qp}"), D(f"b{qp}"), D("state")],
                            writes=[D(f"h{qp}")])
                        em.dve(lambda v, col=col, qp=qp: v.tensor_copy(
                            out=state[:, col:col + 1], in_=h_t[qp][:, T - 1:T]),
                            reads=[D(f"h{qp}")], writes=[D("state")])
                        if sg == 0:
                            em.dve(lambda v, d=d, qp=qp, pp=pp: v.tensor_tensor(
                                out=ys.ap()[:, d, :], in0=h_t[qp][:, :],
                                in1=cbc[pp][:, :], op=ALU.mult),
                                reads=[D(f"h{qp}"), D(f"cbc{pp}")], writes=[D("ys")])
                        else:
                            em.dve(lambda v, qp=qp, pp=pp: v.tensor_tensor(
                                out=b_t[qp][:, :], in0=h_t[qp][:, :],
                                in1=cbc[pp][:, :], op=ALU.mult),
                                reads=[D(f"h{qp}"), D(f"cbc{pp}")],
                                writes=[D(f"b{qp}")])
                            em.dve(lambda v, d=d, qp=qp: v.tensor_tensor(
                                out=ys.ap()[:, d, :], in0=ys.ap()[:, d, :],
                                in1=b_t[qp][:, :], op=ALU.add),
                                reads=[D(f"b{qp}")], writes=[D("ys")])

                for d in range(4):
                    em.dve(lambda v, d=d, xc16=xc16: v.scalar_tensor_tensor(
                        out=caccB[:, :], in0=xc16.ap()[:, d, :], scalar=dp_s[:, d:d + 1],
                        in1=ys.ap()[:, d, :], op0=ALU.mult, op1=ALU.add),
                        reads=[Dxc, D("dp"), D("ys")], writes=[D("caccB")])
                    em.dve(lambda v, d=d, z16=z16: v.tensor_tensor(
                        out=ys.ap()[:, d, :], in0=caccB[:, :],
                        in1=z16.ap()[:, d, :], op=ALU.mult),
                        reads=[D("caccB"), Dz], writes=[D("ys")])
                for j in range(8):
                    em.dma("sp", lambda h, k=k, j=j: h.dma_start(
                        out=a2a_in_l[k].ap()[j:j + 1, :].rearrange(
                            "o (d p t) -> (o p) d t", p=128, d=4),
                        in_=ys.ap()[:, :, (j % 4) * TQ:(j % 4 + 1) * TQ]),
                        reads=[D("ys")], writes=[D(f"a2a_in{k}")])
                if DBG and "i" in DBG2:
                    em.dma("sp", lambda h, k=k: h.dma_start(
                        out=dbg_y2i[k],
                        in_=ys.ap().rearrange("p a b -> p (a b)")),
                        reads=[D("ys")], writes=[D("dbg")])

            def emit_D1a(k):
                for kt in range(16):
                    em.dma("gp", lambda h, k=k, kt=kt: h.indirect_dma_start(
                        out=y2tok.ap()[:, kt, :], out_offset=None,
                        in_=a2a_out_l[k].ap().rearrange("s (c t) -> (s c) t", t=TQ),
                        in_offset=IndirectOffsetOnAxis(slotidx[:, kt:kt + 1], 0),
                        bounds_check=bregs[8 * 512 - 1], oob_is_err=False),
                        reads=[D(f"a2a_out{k}"), D("slotidx")], writes=[D("y2tok")])
                em.dma("gp", lambda h: h.dma_start(
                    out=flush_d.ap(), in_=y2tok.ap()[0:1, 15, 0:16]),
                    writes=[D("y2tok")])
                if DBG and "g" in DBG2:
                    em.dma("sp", lambda h, k=k: h.dma_start(
                        out=dbg_y2g[k],
                        in_=y2tok.ap().rearrange("p a b -> p (a b)")),
                        reads=[D("y2tok")], writes=[D("dbg")])
                for a in range(TPC):
                    tok0 = k * TQ + a * 128
                    em.dma("sp", lambda h, tok0=tok0: h.dma_start(
                        out=xstD.ap(), in_=x_own[tok0:tok0 + 128, :]),
                        writes=[D("xstD")])
                    em.dve(lambda v, a=a: v.tensor_copy(
                        out=xtok16.ap()[:, a, :], in_=xstD.ap()),
                        reads=[D("xstD")], writes=[D("xtok16")])
                    for dt in range(8):
                        em.pe(lambda t, a=a, dt=dt: t.transpose(
                            ptr16b.ap()[:, dt, :],
                            xtok16.ap()[:, a, dt * 128:(dt + 1) * 128],
                            ident16[:, :]),
                            reads=[D("xtok16"), D("ident16")], writes=[D("ptr16b")])
                    em.dve(lambda v, a=a: v.tensor_copy(
                        out=xD.ap()[:, :, a * 128:(a + 1) * 128], in_=ptr16b.ap()),
                        reads=[D("ptr16b")], writes=[D("xD")])
                if DBG and "x" in DBG2:
                    em.dma("sp", lambda h, k=k: h.dma_start(
                        out=dbg_xD[k],
                        in_=xD.ap().rearrange("p a b -> p (a b)")),
                        reads=[D("xD")], writes=[D("dbg")])

            def emit_D1b(k):
                opf = opw_h.ap()
                mmset8(m16, "m16", lambda kk: y2tok[:, kk, :], D("y2tok"), 16,
                       lambda w: opf[:, w * 8 * DM:(w + 1) * 8 * DM], [])
                mmset8(g16, "g16", lambda kk: xD[:, kk, :], D("xD"), 8,
                       lambda w: gw_h.ap(), [],
                       act_fn=AF.Sigmoid, bias=gateb_s, biasdep=D("gateb"))
                mmset8(dbr, "dbr", lambda kk: xD[:, kk, :], D("xD"), 8,
                       lambda w: f1_h.ap(), [],
                       act_fn=AF.Identity, bias=fdb1_s, biasdep=D("fdb1"))

            def emit_D2(k):
                dbrf0 = dbr.ap().rearrange("p a t -> p (a t)")
                combf0 = comb.ap().rearrange("p a t -> p (a t)")
                for a in range(TPC):
                    for dt in range(8):
                        em.pe(lambda t, a=a, dt=dt: t.transpose(
                            ptr16b.ap()[:, dt, :], dbr[:, dt, a * 128:(a + 1) * 128],
                            ident16[:, :]),
                            reads=[D("dbr"), D("ident16")], writes=[D("ptr16b")])
                    em.dve(lambda v, a=a: v.tensor_copy(
                        out=combf0[:, a * DM:(a + 1) * DM],
                        in_=ptr16b.ap().rearrange("p a b -> p (a b)")),
                        reads=[D("ptr16b")], writes=[D("comb")])
                c0sl = [combf0[:, a * DM:(a + 1) * DM] for a in range(TPC)]
                taf0 = t_a.ap().rearrange("p a t -> p (a t)")
                t0sl = [taf0[:, a * DM:(a + 1) * DM] for a in range(TPC)]
                ln_token(c0sl, D("comb"), t0sl, D("t_a"), c0sl, D("comb"),
                         gb=(fggbc, D("fggbc"), fgbbc, D("fgbbc")))
                for a in range(TPC):
                    em.act(lambda s, a=a: s.activation(
                        out=c0sl[a], in_=c0sl[a], func=AF.Relu),
                        reads=[], writes=[D("comb")])
                    for dt in range(8):
                        em.pe(lambda t, a=a, dt=dt: t.transpose(
                            ptr16b.ap()[:, dt, :],
                            combf0[:, a * DM + dt * 128:a * DM + (dt + 1) * 128],
                            ident16[:, :]),
                            reads=[D("comb"), D("ident16")], writes=[D("ptr16b")])
                    em.dve(lambda v, a=a: v.tensor_copy(
                        out=dbr.ap()[:, :, a * 128:(a + 1) * 128], in_=ptr16b.ap()),
                        reads=[D("ptr16b")], writes=[D("dbr")])
                mmset8(y2tok, "y2tok", lambda kk: dbr[:, kk, :], D("dbr"), 8,
                       lambda w: f2_h.ap(), [],
                       act_fn=AF.Identity, bias=fdb2_s, biasdep=D("fdb2"))
                for m in range(8):
                    em.dve(lambda v, m=m: v.tensor_scalar(
                        out=t_a.ap()[:, m, :], in0=y2tok.ap()[:, m, :],
                        scalar1=onef_col[:, 0:1], scalar2=None, op0=ALU.mult),
                        reads=[D("y2tok"), D("onefcol")], writes=[D("t_a")])
                    em.dve(lambda v, m=m: v.scalar_tensor_tensor(
                        out=comb.ap()[:, m, :], in0=m16.ap()[:, m, :],
                        scalar=f_col[:, 0:1], in1=t_a.ap()[:, m, :],
                        op0=ALU.mult, op1=ALU.add),
                        reads=[D("m16"), D("fcol"), D("t_a")], writes=[D("comb")])
                    em.dve(lambda v, m=m: v.tensor_tensor(
                        out=t_a.ap()[:, m, :], in0=xD.ap()[:, m, :],
                        in1=comb.ap()[:, m, :], op=ALU.subtract),
                        reads=[D("xD"), D("comb")], writes=[D("t_a")])
                    em.dve(lambda v, m=m: v.tensor_tensor(
                        out=t_a.ap()[:, m, :], in0=g16.ap()[:, m, :],
                        in1=t_a.ap()[:, m, :], op=ALU.mult),
                        reads=[D("g16")], writes=[D("t_a")])
                    em.dve(lambda v, m=m: v.tensor_tensor(
                        out=comb.ap()[:, m, :], in0=comb.ap()[:, m, :],
                        in1=t_a.ap()[:, m, :], op=ALU.add),
                        reads=[D("t_a")], writes=[D("comb")])
                dbrf = dbr.ap().rearrange("p a t -> p (a t)")
                combf = comb.ap().rearrange("p a t -> p (a t)")
                m16f = m16.ap().rearrange("p a t -> p (a t)")
                for a in range(TPC):
                    for dt in range(8):
                        em.pe(lambda t, a=a, dt=dt: t.transpose(
                            ptr16b.ap()[:, dt, :], comb[:, dt, a * 128:(a + 1) * 128],
                            ident16[:, :]),
                            reads=[D("comb"), D("ident16")], writes=[D("ptr16b")])
                    em.dve(lambda v, a=a: v.tensor_copy(
                        out=dbrf[:, a * DM:(a + 1) * DM],
                        in_=ptr16b.ap().rearrange("p a b -> p (a b)")),
                        reads=[D("ptr16b")], writes=[D("dbr")])
                dsl = [dbrf[:, a * DM:(a + 1) * DM] for a in range(TPC)]
                csl = [combf[:, a * DM:(a + 1) * DM] for a in range(TPC)]
                msl = [m16f[:, a * DM:(a + 1) * DM] for a in range(TPC)]
                ln_token(dsl, D("dbr"), msl, D("m16"), csl, D("comb"),
                         gb=(ngbc, D("ngbc"), nbbc, D("nbbc")))
                for a in range(TPC):
                    em.dve(lambda v, a=a: v.tensor_tensor(
                        out=csl[a], in0=csl[a], in1=xtok16.ap()[:, a, :],
                        op=ALU.add),
                        reads=[D("xtok16")], writes=[D("comb")])
                ln_token(csl, D("comb"), msl, D("m16"), dsl, D("dbr"))
                em.dma("sp", lambda h, k=k: h.dma_start(
                    out=x1tok_d.ap()[k].rearrange("(a p) d -> p a d", p=128),
                    in_=dbr.ap().rearrange("p a t -> p (a t)").rearrange(
                        "p (a d) -> p a d", a=TPC)),
                    reads=[D("dbr")], writes=[D(f"x1tok{k}")])
                for a in range(TPC):
                    for dt in range(8):
                        em.pe(lambda t, a=a, dt=dt: t.transpose(
                            ptr16b.ap()[:, dt, :],
                            dbrf[:, a * DM + dt * 128:a * DM + (dt + 1) * 128],
                            ident16[:, :]),
                            reads=[D("dbr"), D("ident16")], writes=[D("ptr16b")])
                    em.dve(lambda v, a=a: v.tensor_copy(
                        out=comb.ap()[:, :, a * 128:(a + 1) * 128],
                        in_=ptr16b.ap()),
                        reads=[D("ptr16b")], writes=[D("comb")])
                if DBG:
                    em.dma("sp", lambda h, k=k: h.dma_start(
                        out=x1_d.ap()[k].rearrange("(a p) t -> p a t", p=128),
                        in_=comb.ap()),
                        reads=[D("comb")], writes=[D(f"x1_d{k}")])
                bk, bkd = pbank[4], D("pb4")
                for kk in range(8):
                    mm(bk.ap()[0:8, 0:TQ], fgw16[:, kk, :], comb[:, kk, :],
                       kk == 0, kk == 7, [D("fgw16"), D("comb")], [bkd])
                em.dve(lambda v, bk=bk: v.tensor_copy(
                    out=m16.ap().rearrange("p a t -> p (a t)")[0:8, 0:TQ],
                    in_=bk.ap()[0:8, 0:TQ]),
                    reads=[bkd], writes=[D("m16")])
                for a in range(TPC):
                    em.pe(lambda t, a=a: t.transpose(
                        ptr16b.ap()[:, 0, 0:8],
                        m16.ap().rearrange("p a t -> p (a t)")[0:8, a * 128:(a + 1) * 128],
                        ident16[0:8, 0:8]),
                        reads=[D("m16"), D("ident16")], writes=[D("ptr16b")])
                    em.dve(lambda v, k=k, a=a: v.tensor_copy(
                        out=logits_all.ap()[:, k * TPC + a, :],
                        in_=ptr16b.ap()[:, 0, 0:8]),
                        reads=[D("ptr16b")], writes=[D("logits_all")])

            def ln_token(xsl, xdep, sqsl, sqdep, dstsl, dstdep, gb=None,
                         lnc=None):
                na = len(xsl)
                if lnc is None:
                    lnc = lnc_cl
                for a in range(na):
                    em.act(lambda s, a=a: s.activation(
                        out=sqsl[a], in_=xsl[a], func=AF.Square),
                        reads=[xdep], writes=[sqdep])
                    em.dve(lambda v, a=a: v.tensor_reduce(
                        out=lnc[:, 2 * a:2 * a + 1], in_=xsl[a], axis=AX,
                        op=ALU.add),
                        reads=[xdep], writes=[D("lnc")])
                    em.dve(lambda v, a=a: v.tensor_reduce(
                        out=lnc[:, 2 * a + 1:2 * a + 2], in_=sqsl[a], axis=AX,
                        op=ALU.add),
                        reads=[sqdep], writes=[D("lnc")])
                em.dve(lambda v: v.tensor_scalar(
                    out=lnc[:, 0:2 * na], in0=lnc[:, 0:2 * na],
                    scalar1=cst(C_INV1024), scalar2=None, op0=ALU.mult),
                    reads=[D("consts")], writes=[D("lnc")])
                for a in range(na):
                    em.dve(lambda v, a=a: v.tensor_tensor(
                        out=lnc[:, 4 + a:5 + a], in0=lnc[:, 2 * a:2 * a + 1],
                        in1=lnc[:, 2 * a:2 * a + 1], op=ALU.mult),
                        reads=[], writes=[D("lnc")])
                    em.dve(lambda v, a=a: v.tensor_tensor(
                        out=lnc[:, 4 + a:5 + a], in0=lnc[:, 2 * a + 1:2 * a + 2],
                        in1=lnc[:, 4 + a:5 + a], op=ALU.subtract),
                        reads=[], writes=[D("lnc")])
                em.act(lambda s: s.activation(
                    out=lnc[:, 4:4 + na], in_=lnc[:, 4:4 + na], func=AF.Ln,
                    bias=cst(C_EPS)),
                    reads=[D("consts")], writes=[D("lnc")])
                em.act(lambda s: s.activation(
                    out=lnc[:, 4:4 + na], in_=lnc[:, 4:4 + na], func=AF.Exp,
                    scale=cst(C_NEGHALF)),
                    reads=[D("consts")], writes=[D("lnc")])
                for a in range(na):
                    em.dve(lambda v, a=a: v.tensor_tensor(
                        out=lnc[:, 8 + a:9 + a], in0=lnc[:, 2 * a:2 * a + 1],
                        in1=lnc[:, 4 + a:5 + a], op=ALU.mult),
                        reads=[], writes=[D("lnc")])
                em.dve(lambda v: v.tensor_scalar(
                    out=lnc[:, 8:8 + na], in0=lnc[:, 8:8 + na],
                    scalar1=cst(C_NEG1), scalar2=None, op0=ALU.mult),
                    reads=[D("consts")], writes=[D("lnc")])
                for a in range(na):
                    em.act(lambda s, a=a: s.activation(
                        out=dstsl[a], in_=xsl[a], func=AF.Identity,
                        scale=lnc[:, 4 + a:5 + a], bias=lnc[:, 8 + a:9 + a]),
                        reads=[xdep, D("lnc")], writes=[dstdep])
                    if gb is not None:
                        g_t, gdep, b_t2, bdep = gb
                        em.dve(lambda v, a=a: v.tensor_tensor(
                            out=dstsl[a], in0=dstsl[a], in1=g_t.ap()[:, :],
                            op=ALU.mult),
                            reads=[gdep], writes=[dstdep])
                        em.dve(lambda v, a=a: v.tensor_tensor(
                            out=dstsl[a], in0=dstsl[a], in1=b_t2.ap()[:, :],
                            op=ALU.add),
                            reads=[bdep], writes=[dstdep])

            def layer_norm16(src, srcdep, dst, dstdep):
                layer_norm(src, srcdep, dst, dstdep, TQ, t_a, D("t_a"),
                           rows2, D("rows2"), bcst2, D("bcst2"), gb=None,
                           ones_src=onescol16, ones_scr=onescol16)

            def emit_COLL(ar_k, a2a_k):
                if ar_k is not None:
                    em.coll(lambda g, k=ar_k: g.collective_compute(
                        "AllReduce", ALU.add,
                        replica_groups=[[0, 1, 2, 3], [4, 5, 6, 7]],
                        ins=[xdbl_in.ap()[k]], outs=[xdbl_out.ap()[k]]),
                        reads=[D(f"xdbl_in{ar_k}")], writes=[D(f"xdbl_out{ar_k}")])
                if a2a_k is not None:
                    with allow_ici_all_to_all():
                        em.coll(lambda g, k=a2a_k: g.collective_compute(
                            "AllToAll", ALU.bypass,
                            replica_groups=[list(range(8))],
                            ins=[a2a_in_l[k].ap()], outs=[a2a_out_l[k].ap()]),
                            reads=[D(f"a2a_in{a2a_k}")],
                            writes=[D(f"a2a_out{a2a_k}")], kind="a2a")

            import os as _os3
            SFENCE = _os3.environ.get("KFENCE", "")
            for j in range(NCH + 4):
                if j < NCH:
                    emit_A(j)
                if "1" in SFENCE:
                    em.set_fence()
                if 0 <= j - 1 < NCH:
                    emit_B1(j - 1)
                if 0 <= j - 4 < NCH:
                    emit_D2(j - 4)
                if 0 <= j - 3 < NCH:
                    emit_D1a(j - 3)
                if "2" in SFENCE:
                    em.set_fence()
                if 0 <= j - 1 < NCH:
                    emit_B2(j - 1)
                if "3" in SFENCE:
                    em.set_fence()
                if 0 <= j - 3 < NCH:
                    emit_D1b(j - 3)
                emit_COLL(j if j < NCH else None,
                          j - 1 if 0 <= j - 1 < NCH else None)
        em.set_fence()

        with ExitStack() as cR:
            maskE = sbt(cR, "maskE", [8, NBLK * 128], F16)
            cumE = sbt(cR, "cumE", [8, NBLK * 128], F16)
            msegE = sbt(cR, "msegE", [8, NBLK * 128], F16)

            for j in range(NBLK):
                em.dve(lambda v, j=j: v.tensor_tensor(
                    out=logits_all.ap()[:, j, :], in0=logits_all.ap()[:, j, :],
                    in1=ffngb_s[:, :], op=ALU.add),
                    reads=[D("ffngb")], writes=[D("logits_all")])
                em.dve(lambda v, j=j: v.tensor_reduce(
                    out=r0[:, j:j + 1], in_=logits_all.ap()[:, j, :], axis=AX,
                    op=ALU.max),
                    reads=[D("logits_all")], writes=[D("r0")])
                em.dve(lambda v, j=j: v.tensor_scalar(
                    out=mask_all.ap()[:, j, :], in0=logits_all.ap()[:, j, :],
                    scalar1=r0[:, j:j + 1], scalar2=None, op0=ALU.is_equal),
                    reads=[D("logits_all"), D("r0")], writes=[D("mask_all")])
                em.dve(lambda v, j=j: v.tensor_scalar(
                    out=r0[:, j:j + 1], in0=r0[:, j:j + 1], scalar1=cst(C_NEG1),
                    scalar2=None, op0=ALU.mult),
                    reads=[D("consts")], writes=[D("r0")])
                em.act(lambda s, j=j: s.activation(
                    out=r1[:, j * 8:(j + 1) * 8], in_=logits_all.ap()[:, j, :],
                    func=AF.Exp, bias=r0[:, j:j + 1]),
                    reads=[D("logits_all"), D("r0")], writes=[D("r1")])
            em.dve(lambda v: v.tensor_reduce(
                out=r0[:, 0:NBLK],
                in_=r1.ap()[:, 0:NBLK * 8].rearrange("p (a b) -> p a b", b=8),
                axis=AX, op=ALU.add),
                reads=[D("r1")], writes=[D("r0")])
            em.dve(lambda v: v.reciprocal(out=gprob_all[:, :], in_=r0[:, 0:NBLK]),
                   reads=[D("r0")], writes=[D("gprob")])
            for j in range(NBLK):
                em.dve(lambda v, j=j: v.tensor_tensor(
                    out=r1[:, j * 8:(j + 1) * 8], in0=mask_all.ap()[:, j, :],
                    in1=evals_s[:, :], op=ALU.mult),
                    reads=[D("mask_all"), D("evals")], writes=[D("r1")])
            em.dve(lambda v: v.tensor_reduce(
                out=r0[:, 0:NBLK],
                in_=r1.ap()[:, 0:NBLK * 8].rearrange("p (a b) -> p a b", b=8),
                axis=AX, op=ALU.add),
                reads=[D("r1")], writes=[D("r0")])
            for j in range(NBLK):
                em.pe(lambda t, j=j: t.transpose(
                    ptr16.ap()[0:8, 0, 0:128], mask_all.ap()[:, j, :], ident16[:, :]),
                    reads=[D("mask_all"), D("ident16")], writes=[D("ptr16")])
                em.dve(lambda v, j=j: v.tensor_copy(
                    out=maskE[0:8, j * 128:(j + 1) * 128],
                    in_=ptr16.ap()[0:8, 0, 0:128]),
                    reads=[D("ptr16")], writes=[D("maskE")])
            em.dve(lambda v: v.memset(msegE[:, :], 1.0), writes=[D("mseg")])
            for j in range(NBLK):
                em.dve(lambda v, j=j: v.memset(msegE[:, j * 128:j * 128 + 1], 0.0),
                       writes=[D("mseg")])
            em.dve(lambda v: v.tensor_tensor_scan(
                out=cumE[:, :], data0=msegE[:, :], data1=maskE[:, :],
                initial=0.0, op0=ALU.mult, op1=ALU.add),
                reads=[D("mseg"), D("maskE")], writes=[D("cumE")])
            em.dve(lambda v: v.tensor_copy(
                out=msegE.ap()[0:8, 0:NBLK],
                in_=cumE.ap()[0:8, :].rearrange("p (a b) -> p a b", b=128)[:, :, 127:128].rearrange("p a b -> p (a b)")),
                reads=[D("cumE")], writes=[D("mseg")])
            em.dve(lambda v: v.tensor_tensor(out=cumE[:, :], in0=cumE[:, :],
                                             in1=maskE[:, :], op=ALU.subtract),
                   reads=[D("maskE")], writes=[D("cumE")])
            em.pe(lambda t: t.transpose(ptr16b.ap()[0:NBLK, 0, 0:8],
                                        msegE[0:8, 0:NBLK], ident16[0:8, 0:8]),
                  reads=[D("mseg"), D("ident16")], writes=[D("ptr16b")])
            em.dve(lambda v: v.tensor_copy(out=off_s[0:NBLK, 0:8],
                                           in_=ptr16b.ap()[0:NBLK, 0, 0:8]),
                   reads=[D("ptr16b")], writes=[D("offs")])
            em.dma("sp", lambda h: h.dma_start(out=cnt_in.ap(), in_=off_s[0:NBLK, 0:8]),
                   reads=[D("offs")], writes=[D("cnt_in")])
            em.coll(lambda g: g.collective_compute(
                "AllGather", ALU.bypass, replica_groups=[list(range(8))],
                ins=[cnt_in.ap()], outs=[cnt_out.ap()]),
                reads=[D("cnt_in")], writes=[D("cnt_out")], kind="ag")
            em.dve(lambda v: v.memset(gat_s.ap().rearrange("p a b -> p (a b)"), 0.0),
                   writes=[D("gat")])
            if GBLK % 128 == 0:
                em.dma("sp", lambda h: h.dma_start(
                    out=gat_s.ap(),
                    in_=cnt_out.ap().rearrange("(a p) b -> p a b", p=128)),
                    reads=[D("cnt_out")], writes=[D("gat")])
            else:
                em.dma("sp", lambda h: h.dma_start(
                    out=gat_s.ap()[0:GBLK, 0, :], in_=cnt_out.ap()),
                    reads=[D("cnt_out")], writes=[D("gat")])
            bk, bkd = pbank[4], D("pb4")
            for kk in range(NKORD):
                mm(bk.ap()[0:NBLK, 0:8], ordm_s[:, kk, :], gat_s[:, kk, :],
                   kk == 0, kk == NKORD - 1, [D("ordm"), D("gat")], [bkd])
            em.dve(lambda v, bk=bk: v.tensor_copy(out=off_s[0:NBLK, 0:8],
                                                  in_=bk.ap()[0:NBLK, 0:8]),
                   reads=[bkd], writes=[D("offs")])
            em.dma("sp", lambda h: h.dma_start(
                out=off_rt.ap()[0:1, 0:NBLK * 8],
                in_=off_s[0:NBLK, 0:8].rearrange("a b -> (a b)")[None, :]
                if False else off_s[0:NBLK, 0:8]),
                reads=[D("offs")], writes=[D("offrt")])
            offl = sbt(cR, "offl", [1, 32 * 8], F32)
            em.dma("sp", lambda h: h.dma_start(out=offl[0:1, 0:NBLK * 8],
                                               in_=off_rt.ap()[0:1, 0:NBLK * 8]),
                   reads=[D("offrt")], writes=[D("offl")])
            for j in range(NBLK):
                bko, bkod = pbank[3], D("pb3")
                mm(bko.ap()[:, 0:8], ones32[:, :], offl[0:1, j * 8:(j + 1) * 8],
                   True, True, [D("ones32"), D("offl")], [bkod])
                em.dve(lambda v, bko=bko: v.tensor_copy(out=r2[:, :],
                                                        in_=bko.ap()[:, 0:8]),
                       reads=[bkod], writes=[D("r2")])
                em.pe(lambda t, j=j: t.transpose(
                    ptr16b.ap()[:, 1, 0:8], cumE[0:8, j * 128:(j + 1) * 128],
                    ident16[0:8, 0:8]),
                    reads=[D("cumE"), D("ident16")], writes=[D("ptr16b")])
                em.dve(lambda v: v.tensor_copy(out=r3[:, :], in_=ptr16b.ap()[:, 1, 0:8]),
                       reads=[D("ptr16b")], writes=[D("r3")])
                em.dve(lambda v, j=j: v.tensor_tensor(
                    out=r3[:, :], in0=r3[:, :], in1=mask_all.ap()[:, j, :],
                    op=ALU.mult),
                    reads=[D("mask_all")], writes=[D("r3")])
                em.dve(lambda v, j=j: v.tensor_tensor(
                    out=r2[:, :], in0=r2[:, :], in1=mask_all.ap()[:, j, :],
                    op=ALU.mult),
                    reads=[D("mask_all")], writes=[D("r2")])
                em.dve(lambda v: v.tensor_tensor(out=r2[:, :], in0=r2[:, :],
                                                 in1=r3[:, :], op=ALU.add),
                       reads=[D("r3")], writes=[D("r2")])
                em.dve(lambda v, j=j: v.tensor_reduce(
                    out=r1[:, j:j + 1], in_=r2[:, :], axis=AX, op=ALU.add),
                    reads=[D("r2")], writes=[D("r1")])
                em.dve(lambda v, j=j: v.tensor_scalar(
                    out=r3[:, 0:1], in0=r1[:, j:j + 1], scalar1=cst(C_CAP),
                    scalar2=None, op0=ALU.is_ge),
                    reads=[D("r1"), D("consts")], writes=[D("r3")])
                em.dve(lambda v, j=j: v.scalar_tensor_tensor(
                    out=r2[:, 0:1], in0=r0[:, j:j + 1], scalar=cst(C_CAP),
                    in1=r1[:, j:j + 1], op0=ALU.mult, op1=ALU.add),
                    reads=[D("r0"), D("consts"), D("r1")], writes=[D("r2")])
                em.dve(lambda v: v.scalar_tensor_tensor(
                    out=r2[:, 0:1], in0=r3[:, 0:1], scalar=cst(C_BIG),
                    in1=r2[:, 0:1], op0=ALU.mult, op1=ALU.add),
                    reads=[D("r3"), D("consts")], writes=[D("r2")])
                em.dve(lambda v, j=j: v.tensor_copy(out=idx_all[:, j:j + 1],
                                                    in_=r2[:, 0:1]),
                       reads=[D("r2")], writes=[D("idx")])
        em.set_fence()

        with ExitStack() as c3:
            xtk = sbt(c3, "xtk", [128, 2, DM], F16)
            estage = sbt(c3, "estage", [128, FF], F32)
            wexp16 = sbt(c3, "wexp16", [128, 32 * DM], F16)
            h16 = sbt(c3, "h16", [128, 32, CAP], F16)
            xbT = sbt(c3, "xbT", [128, 8, CAP], F16)
            disp16 = sbt(c3, "disp16", [128, 5, DM], F16)
            eo16 = sbt(c3, "eo16", [128, 8, CAP], F16)
            eost = sbt(c3, "eost", [128, 8, 128], F16)
            ge_a = sbt(c3, "ge_a", [128, CAP], F32)
            ge_b = sbt(c3, "ge_b", [128, CAP], F32)

            em.dve(lambda v: v.memset(xtk.ap().rearrange("p a b -> p (a b)"), 0.0),
                   writes=[D("xtk")])
            for i in range(NE * CAP // 256):
                em.dma("gp", lambda h, i=i: h.dma_start(
                    out=disp_d.ap()[i * 256:(i + 1) * 256, :].rearrange(
                        "(a p) d -> p a d", p=128),
                    in_=xtk.ap().rearrange("p a d -> p a d")),
                    reads=[D("xtk")], writes=[D("disp")])
            for k in range(NCH):
                em.dma("sp", lambda h, k=k: h.dma_start(
                    out=xtk.ap()[:, 0:TPC, :],
                    in_=x1tok_d.ap()[k].rearrange("(a p) d -> p a d", p=128)),
                    reads=[D(f"x1tok{k}")], writes=[D("xtk")])
                for a in range(TPC):
                    j = k * TPC + a
                    em.dma("gp", lambda h, a=a, j=j: h.indirect_dma_start(
                        out=disp_d.ap(),
                        out_offset=IndirectOffsetOnAxis(idx_all[:, j:j + 1], 0),
                        in_=xtk[:, a, :], in_offset=None,
                        bounds_check=bregs[NE * CAP - 1], oob_is_err=False),
                        reads=[D("xtk"), D("idx")], writes=[D("disp")])
            em.dma("gp", lambda h: h.dma_start(
                out=gpfl.ap(), in_=disp_d.ap()[NE * CAP - 1:NE * CAP, 0:16]),
                writes=[D("disp")])
            em.coll(lambda g: g.collective_compute(
                "ReduceScatter", ALU.add, replica_groups=[list(range(8))],
                ins=[disp_d.ap()], outs=[dispm_d.ap()]),
                reads=[D("disp")], writes=[D("dispm")], kind="rs")

            em.dma("sp", lambda h: h.dma_start(
                out=disp16.ap(),
                in_=dispm_d.ap().rearrange("(a p) d -> p a d", p=128)),
                reads=[D("dispm")], writes=[D("disp16")])
            for a in range(5):
                for dt in range(8):
                    em.pe(lambda t, a=a, dt=dt: t.transpose(
                        ptr16.ap()[:, dt, :], disp16[:, a, dt * 128:(dt + 1) * 128],
                        ident16[:, :]),
                        reads=[D("disp16"), D("ident16")], writes=[D("ptr16")])
                em.dve(lambda v, a=a: v.tensor_copy(
                    out=xbT.ap()[:, :, a * 128:(a + 1) * 128], in_=ptr16.ap()),
                    reads=[D("ptr16")], writes=[D("xbT")])
            w1v = wexp16.ap().rearrange("p (a b) -> p a b", a=8)
            for hh in range(8):
                em.dma("act", lambda h, hh=hh: h.dma_start(
                    out=estage.ap(),
                    in_=w1e.rearrange("p a b -> p (a b)")[:, hh * FF:(hh + 1) * FF]),
                    writes=[D("estage")])
                em.dve(lambda v, hh=hh: v.tensor_copy(
                    out=wexp16.ap()[:, hh * FF:(hh + 1) * FF],
                    in_=estage.ap()),
                    reads=[D("estage")], writes=[D("wexp")])
            for m in range(32):
                bka, bkad = pbank[m % 2], D(f"pb{m % 2}")
                bkb, bkbd = pbank[2 + m % 2], D(f"pb{2 + m % 2}")
                for kk in range(8):
                    mm(bka.ap()[:, 0:320], w1v[:, kk, m * 128:(m + 1) * 128],
                       xbT[:, kk, 0:320], kk == 0, kk == 7,
                       [D("wexp"), D("xbT")], [bkad])
                for kk in range(8):
                    mm(bkb.ap()[:, 0:320], w1v[:, kk, m * 128:(m + 1) * 128],
                       xbT[:, kk, 320:640], kk == 0, kk == 7,
                       [D("wexp"), D("xbT")], [bkbd])
                em.act(lambda s, m=m, bka=bka: s.activation(
                    out=ge_a[:, 0:320], in_=bka.ap()[:, 0:320], func=AF.Identity,
                    bias=b1_s[:, m:m + 1]),
                    reads=[bkad, D("b1")], writes=[D("ge_a")])
                em.act(lambda s, m=m, bkb=bkb: s.activation(
                    out=ge_a[:, 320:640], in_=bkb.ap()[:, 0:320], func=AF.Identity,
                    bias=b1_s[:, m:m + 1]),
                    reads=[bkbd, D("b1")], writes=[D("ge_a")])
                em.act(lambda s: s.activation(out=ge_b[:, :], in_=ge_a[:, :],
                                              func=AF.Square),
                       reads=[D("ge_a")], writes=[D("ge_b")])
                em.dve(lambda v: v.tensor_tensor(out=ge_b[:, :], in0=ge_b[:, :],
                                                 in1=ge_a[:, :], op=ALU.mult),
                       reads=[D("ge_a")], writes=[D("ge_b")])
                em.dve(lambda v: v.scalar_tensor_tensor(
                    out=ge_b[:, :], in0=ge_b[:, :], scalar=cst(C_G1), in1=ge_a[:, :],
                    op0=ALU.mult, op1=ALU.add),
                    reads=[D("ge_a"), D("consts")], writes=[D("ge_b")])
                em.act(lambda s: s.activation(out=ge_b[:, :], in_=ge_b[:, :],
                                              func=AF.Tanh, scale=cst(C_G2)),
                       reads=[D("consts")], writes=[D("ge_b")])
                em.dve(lambda v: v.tensor_scalar(out=ge_b[:, :], in0=ge_b[:, :],
                                                 scalar1=cst(C_ONE), scalar2=None,
                                                 op0=ALU.add),
                       reads=[D("consts")], writes=[D("ge_b")])
                em.dve(lambda v, m=m: v.scalar_tensor_tensor(
                    out=h16.ap()[:, m, :], in0=ge_b[:, :], scalar=cst(C_HALF),
                    in1=ge_a[:, :], op0=ALU.mult, op1=ALU.mult),
                    reads=[D("ge_b"), D("ge_a"), D("consts")], writes=[D("h16")])
            w2v = wexp16.ap().rearrange("p (a b) -> p a b", a=32)
            for hh in range(8):
                em.dma("act", lambda h, hh=hh: h.dma_start(
                    out=estage.ap(),
                    in_=w2e.rearrange("p a b -> p (a b)")[:, hh * 4 * DM:(hh + 1) * 4 * DM]),
                    writes=[D("estage")])
                em.dve(lambda v, hh=hh: v.tensor_copy(
                    out=wexp16.ap()[:, hh * 4 * DM:(hh + 1) * 4 * DM],
                    in_=estage.ap()),
                    reads=[D("estage")], writes=[D("wexp")])
            for m in range(8):
                bka, bkad = pbank[m % 2], D(f"pb{m % 2}")
                bkb, bkbd = pbank[2 + m % 2], D(f"pb{2 + m % 2}")
                for kk in range(32):
                    mm(bka.ap()[:, 0:320], w2v[:, kk, m * 128:(m + 1) * 128],
                       h16[:, kk, 0:320], kk == 0, kk == 31,
                       [D("wexp"), D("h16")], [bkad])
                for kk in range(32):
                    mm(bkb.ap()[:, 0:320], w2v[:, kk, m * 128:(m + 1) * 128],
                       h16[:, kk, 320:640], kk == 0, kk == 31,
                       [D("wexp"), D("h16")], [bkbd])
                em.act(lambda s, m=m, bka=bka: s.activation(
                    out=eo16.ap()[:, m, 0:320], in_=bka.ap()[:, 0:320], func=AF.Identity,
                    bias=b2_s[:, m:m + 1]),
                    reads=[bkad, D("b2")], writes=[D("eo16")])
                em.act(lambda s, m=m, bkb=bkb: s.activation(
                    out=eo16.ap()[:, m, 320:640], in_=bkb.ap()[:, 0:320], func=AF.Identity,
                    bias=b2_s[:, m:m + 1]),
                    reads=[bkbd, D("b2")], writes=[D("eo16")])
            for a in range(5):
                for dt in range(8):
                    em.pe(lambda t, a=a, dt=dt: t.transpose(
                        ptr16.ap()[:, dt, :], eo16[:, dt, a * 128:(a + 1) * 128],
                        ident16[:, :]),
                        reads=[D("eo16"), D("ident16")], writes=[D("ptr16")])
                em.dve(lambda v: v.tensor_copy(
                    out=eost.ap().rearrange("p a b -> p (a b)"),
                    in_=ptr16.ap().rearrange("p a b -> p (a b)")),
                    reads=[D("ptr16")], writes=[D("eost")])
                em.dma("sp", lambda h, a=a: h.dma_start(
                    out=agin_d.ap()[a * 128:(a + 1) * 128, :].rearrange(
                        "p (dt c) -> p dt c", dt=8),
                    in_=eost.ap()),
                    reads=[D("eost")], writes=[D("agin")])
            em.coll(lambda g: g.collective_compute(
                "AllGather", ALU.bypass, replica_groups=[list(range(8))],
                ins=[agin_d.ap()], outs=[agout_d.ap()]),
                reads=[D("agin")], writes=[D("agout")], kind="ag")
        em.set_fence()
        pb03.close()

        with ExitStack() as c4:
            moe16 = sbt(c4, "moe16", [128, NBLK, DM], F16)
            x1D = sbt(c4, "x1D", [128, TPC, DM], F16)
            x2 = sbt(c4, "x2", [128, TPC, DM], F32)
            scr4 = sbt(c4, "scr4", [128, DM], F32)
            lnc4 = sbt(c4, "lnc4", [128, 16], F32)
            ostage = sbt(c4, "ostage", [128, TPC, DM], F32)

            em.dve(lambda v: v.memset(moe16.ap().rearrange("p a b -> p (a b)"), 0.0),
                   writes=[D("moe16")])
            for j in range(NBLK):
                em.dma("gp", lambda h, j=j: h.indirect_dma_start(
                    out=moe16.ap()[:, j, :], out_offset=None,
                    in_=agout_d.ap(),
                    in_offset=IndirectOffsetOnAxis(idx_all[:, j:j + 1], 0),
                    bounds_check=bregs[NE * CAP - 1], oob_is_err=False),
                    reads=[D("agout"), D("idx")], writes=[D("moe16")])
            em.dma("gp", lambda h: h.dma_start(
                out=flush_d.ap(), in_=moe16.ap()[0:1, NBLK - 1, 0:16]),
                writes=[D("moe16")])
            for j in range(NBLK):
                em.dve(lambda v, j=j: v.tensor_scalar(
                    out=moe16.ap()[:, j, :], in0=moe16.ap()[:, j, :],
                    scalar1=gprob_all[:, j:j + 1], scalar2=None, op0=ALU.mult),
                    reads=[D("gprob")], writes=[D("moe16")])
            for k in range(NCH):
                em.dma("sp", lambda h, k=k: h.dma_start(
                    out=x1D.ap(),
                    in_=x1tok_d.ap()[k].rearrange("(a p) d -> p a d", p=128)),
                    reads=[D(f"x1tok{k}")], writes=[D("x1D")])
                for a in range(TPC):
                    j = k * TPC + a
                    em.dve(lambda v, j=j, a=a: v.tensor_tensor(
                        out=x2.ap()[:, a, :], in0=x1D.ap()[:, a, :],
                        in1=moe16.ap()[:, j, :], op=ALU.add),
                        reads=[D("x1D"), D("moe16")], writes=[D("x2")])
                ln_token([x2.ap()[:, a, :] for a in range(TPC)], D("x2"),
                         [scr4.ap()[:, :]] * TPC, D("scr4"),
                         [ostage.ap()[:, a, :] for a in range(TPC)], D("ostage"),
                         lnc=lnc4)
                em.dma("sp", lambda h, k=k: h.dma_start(
                    out=out_core[k * TQ:(k + 1) * TQ, :].rearrange(
                        "(a p) d -> p a d", p=128),
                    in_=ostage.ap()),
                    reads=[D("ostage")], writes=[D("outw")])

        if DBG:
            for k in range(NCH):
                em.dma("sp", lambda h, k=k: h.dma_start(out=dbg_x1[k], in_=x1_d.ap()[k]),
                       reads=[D(f"x1_d{k}")], writes=[D("dbg")])
            em.dma("sp", lambda h: h.dma_start(out=dbg_idx, in_=idx_all[:, :]),
                   reads=[D("idx")], writes=[D("dbg")])
            em.dma("sp", lambda h: h.dma_start(out=dbg_gp, in_=gprob_all[:, :]),
                   reads=[D("gprob")], writes=[D("dbg")])

        global LAST_EM
        LAST_EM = dict(em.cnt)

        fin = {n: v for n, v in em.cnt.items() if v > 0}

        def fin_close(h, wl=fin):
            for s, v in sorted(wl.items()):
                h.wait_ge(em.sems[s], v)
            return h.nop()
        em.streams["gp"].append(fin_close)

        with nc.Block() as block:
            @block.sync
            def _(sync):
                for f in em.streams["sp"]:
                    f(sync)

            @block.gpsimd
            def _(g):
                for f in em.streams["gp"]:
                    f(g)

            @block.vector
            def _(vec):
                for f in em.streams["dve"]:
                    f(vec)

            @block.scalar
            def _(sc):
                for f in em.streams["act"]:
                    f(sc)

            @block.tensor
            def _(t):
                for f in em.streams["pe"]:
                    f(t)

    return nc


def _prep_core_inputs(c, L, T, inputs):
    b, q = c // 4, c % 4
    NCH = L // T
    TQ = T // 4
    TPC = TQ // 128
    NBLK = NCH * TPC
    GBLK = 8 * NBLK
    NKORD = (GBLK + 127) // 128
    f32 = np.float32

    x = np.asarray(inputs["L0_attn_x"])
    xb = np.ascontiguousarray(x[b])
    own = np.concatenate([xb[k * T + q * TQ:(k * T) + (q + 1) * TQ] for k in range(NCH)], axis=0)

    def tile_kT(wT, ktiles):
        K, M = wT.shape
        assert K == ktiles * 128
        return np.ascontiguousarray(wT.reshape(ktiles, 128, M).transpose(1, 0, 2))

    ipw = np.asarray(inputs["L0_attn_in_proj_w"])
    rows = np.concatenate([ipw[512 * q:512 * (q + 1)], ipw[DI + 512 * q:DI + 512 * (q + 1)]])
    inprojT = tile_kT(np.ascontiguousarray(rows.T), 8)
    xpw = np.asarray(inputs["L0_attn_x_proj_w"])
    xprojT = tile_kT(np.ascontiguousarray(xpw[:, 512 * q:512 * (q + 1)].T), 4)
    dtw = np.asarray(inputs["L0_attn_dt_proj_w"])
    dtwT = np.ascontiguousarray(dtw[512 * q:512 * (q + 1)].T)

    def shard_cols(vec):
        return np.ascontiguousarray(np.asarray(vec)[512 * q:512 * (q + 1)].reshape(4, 128).T)

    convw_ = np.asarray(inputs["L0_attn_conv_w"])[:, 0, :]
    convw_ = convw_[512 * q:512 * (q + 1)].reshape(4, 128, 4).transpose(1, 0, 2).reshape(128, 16)
    convb_ = shard_cols(inputs["L0_attn_conv_b"])
    dtb_ = shard_cols(inputs["L0_attn_dt_proj_b"])
    dp_ = shard_cols(inputs["L0_attn_Dp"])
    alog_ = np.asarray(inputs["L0_attn_A_log"])[512 * q:512 * (q + 1)]
    alog_ = alog_.reshape(4, 128, 16).transpose(1, 0, 2).reshape(128, 64)

    opwT = tile_kT(np.ascontiguousarray(np.asarray(inputs["L0_attn_out_proj_w"]).T), 16)
    gateT = tile_kT(np.ascontiguousarray(np.asarray(inputs["L0_attn_gate_w"]).T), 8)
    fd1T = tile_kT(np.ascontiguousarray(np.asarray(inputs["L0_attn_fd_w1"]).T), 8)
    fd2T = tile_kT(np.ascontiguousarray(np.asarray(inputs["L0_attn_fd_w2"]).T), 8)

    def mcol(vec, mt=8):
        return np.ascontiguousarray(np.asarray(vec).reshape(mt, 128).T)

    ffngw = tile_kT(np.asarray(inputs["L0_ffn_gate_w"]), 8)
    ffngb_bc = np.broadcast_to(np.asarray(inputs["L0_ffn_gate_b"])[None, :], (128, 8))
    w1 = np.asarray(inputs["L0_ffn_w1"])[c]
    w2 = np.asarray(inputs["L0_ffn_w2"])[c]
    w1t = tile_kT(w1, 8)
    w2t = tile_kT(w2, 32)
    b1_ = mcol(np.asarray(inputs["L0_ffn_b1"])[c], 32)
    b2_ = mcol(np.asarray(inputs["L0_ffn_b2"])[c], 8)

    consts = np.zeros((128, NCONST), f32)
    vals = {C_NEG1: -1.0, C_ONE: 1.0, C_EPS: 1e-5, C_NEGHALF: -0.5,
            C_INV1024: 1.0 / 1024.0, C_CAP: float(CAP), C_BIG: 1.0e9,
            C_G1: 0.044715, C_G2: 0.7978845608028654, C_HALF: 0.5, C_ZERO: 0.0}
    for col, v in vals.items():
        consts[:, col] = v
    evals = np.broadcast_to(np.arange(8, dtype=f32)[None, :], (128, 8))

    def key(cc, jj):
        bb, qq = cc // 4, cc % 4
        kk, aa = jj // TPC, jj % TPC
        return ((bb * NCH + kk) * 4 + qq) * TPC + aa

    ordm = np.zeros((128, NKORD, NBLK), f32)
    myk = [key(c, j) for j in range(NBLK)]
    for cc in range(8):
        for jj in range(NBLK):
            g = cc * NBLK + jj
            kg = key(cc, jj)
            for j in range(NBLK):
                if kg < myk[j]:
                    ordm[g % 128, g // 128, j] = 1.0

    slotidx = np.zeros((128, 16), np.int32)
    for kt in range(16):
        s_ch, dd = kt // 4, kt % 4
        for p in range(128):
            slotidx[p, kt] = (4 * b + s_ch) * 512 + dd * 128 + p
    return dict(
        x_rows=xb, x_own=np.ascontiguousarray(own), slotidx_i=slotidx,
        inprojT=inprojT, xprojT=xprojT, dtwT=dtwT,
        convw=convw_, convb=convb_, dtb=dtb_, dpv=dp_, alog=alog_,
        opwT=opwT, gateT=gateT, fd1T=fd1T, fd2T=fd2T,
        gateb=mcol(inputs["L0_attn_gate_b"]), fdb1=mcol(inputs["L0_attn_fd_b1"]),
        fdb2=mcol(inputs["L0_attn_fd_b2"]), fdlng=mcol(inputs["L0_attn_fd_ln_g"]),
        fdlnb=mcol(inputs["L0_attn_fd_ln_b"]), normg=mcol(inputs["L0_attn_norm_g"]),
        normb=mcol(inputs["L0_attn_norm_b"]),
        bal=np.asarray(inputs["L0_attn_balance_factor"], f32).reshape(1, 1),
        ffngw=ffngw, ffngb=np.ascontiguousarray(ffngb_bc),
        w1e=w1t, b1e=b1_, w2e=w2t, b2e=b2_,
        ident16_i=np.eye(128, dtype=np.float16), ident32_i=np.eye(128, dtype=f32),
        normgr=np.asarray(inputs["L0_attn_norm_g"]).astype(np.float16)[None, :],
        fdlngr=np.asarray(inputs["L0_attn_fd_ln_g"]).astype(np.float16)[None, :],
        fdlnbr=np.asarray(inputs["L0_attn_fd_ln_b"]).astype(np.float16)[None, :],
        normbr=np.asarray(inputs["L0_attn_norm_b"]).astype(np.float16)[None, :],
        consts_i=consts, evals_i=np.ascontiguousarray(evals),
        ordm_i=ordm,
    )


def kernel(**inputs):
    L = int(np.asarray(inputs["L0_attn_x"]).shape[1])
    T = 1024 if L % 1024 == 0 else 512
    NCH = L // T
    TQ = T // 4
    nc = build_program(L, T)
    in_maps = [_prep_core_inputs(c, L, T, inputs) for c in range(8)]
    res = run_bass_kernel_spmd(nc, in_maps, core_ids=list(range(8)))
    B = int(np.asarray(inputs["L0_attn_x"]).shape[0])
    out = np.zeros((B, L, DM), np.float32)
    for c in range(8):
        b, q = c // 4, c % 4
        oc = res.results[c]["out_core"]
        for k in range(NCH):
            out[b, k * T + q * TQ: k * T + (q + 1) * TQ] = oc[k * TQ:(k + 1) * TQ]
    return out
